# version 01
# baseline (speedup 1.0000x reference)
import math
from contextlib import ExitStack

import numpy as np
import ml_dtypes

import concourse.bass as bass
import concourse.mybir as mybir
from concourse.bass_utils import run_bass_kernel_spmd

F32 = mybir.dt.float32
BF16 = mybir.dt.bfloat16
AF = mybir.ActivationFunctionType
ALU = mybir.AluOpType
AX = mybir.AxisListType

B, SEQ, D = 2, 8192, 2048
TOK, TB, NDC = 2048, 16, 16
NH, HD = 16, 128
E0 = 8
H1, NHC1, HS = 1408, 11, 2816
SCALE = 1.0 / math.sqrt(HD)
GT = 64
BF = ml_dtypes.bfloat16

WORK_COLS = 24576
SM_COLS = 6144

ENGS = ("pe", "dv", "ac", "gp", "sp")


class S:
    __slots__ = ("name", "count")

    def __init__(self, name):
        self.name, self.count = name, 0


class Prog:
    def __init__(self, nc):
        self.nc = nc
        self.stream = {e: [] for e in ENGS}
        self.sems = {}
        self.hwm = {e: {} for e in ENGS}

    def sem(self, name):
        if name not in self.sems:
            self.sems[name] = S(name)
        return self.sems[name]

    def op(self, eng, method, *args, waits=(), inc=None, inc_n=1, **kwargs):
        filt = []
        for s, v in waits:
            if v is None or v <= 0:
                continue
            if v > self.hwm[eng].get(s.name, 0):
                filt.append((s, v))
                self.hwm[eng][s.name] = v
        self.stream[eng].append((method, args, kwargs, filt, inc, inc_n))
        if inc is not None:
            inc.count += inc_n
            return inc.count
        return None

    def dma(self, eng, out, in_, sem, waits=(), **kw):
        return self.op(eng, "dma_start", waits=waits, inc=sem, inc_n=16,
                       out=out, in_=in_, **kw)

    def replay(self, engname, engine, handles):
        for method, args, kwargs, waits, inc, inc_n in self.stream[engname]:
            for s, v in waits:
                engine.wait_ge(handles[s.name], v)
            if callable(method):
                inst = method(engine)
            else:
                inst = getattr(engine, method)(*args, **kwargs)
            if inc is not None:
                assert inst is not None
                inst.then_inc(handles[inc.name], inc_n)

    def finalize(self, stack, final_waits):
        nc = self.nc
        handles = {name: stack.enter_context(nc.semaphore(name))
                   for name in self.sems}
        block = stack.enter_context(nc.Block())
        p, fw = self, list(final_waits)

        @block.tensor
        def _(t):
            p.replay("pe", t, handles)

        @block.vector
        def _(v):
            p.replay("dv", v, handles)

        @block.scalar
        def _(s):
            p.replay("ac", s, handles)

        @block.gpsimd
        def _(g):
            p.replay("gp", g, handles)

        @block.sync
        def _(sy):
            p.replay("sp", sy, handles)
            for s, v in fw:
                sy.wait_ge(handles[s.name], v)


class Bufs:

    def __init__(self, nc, stack):
        self.nc, self.stack, self.t = nc, stack, {}
        self._work = None
        self._sm = None

    def sb(self, name, shape, dtype):
        if name not in self.t:
            self.t[name] = self.stack.enter_context(
                self.nc.sbuf_tensor(name, shape, dtype))
        return self.t[name]

    def ps(self, name, shape, dtype):
        if name not in self.t:
            self.t[name] = self.stack.enter_context(
                self.nc.psum_tensor(name, shape, dtype))
        return self.t[name]

    def work(self, c0, ncols, nrow=128):
        if self._work is None:
            self._work = self.stack.enter_context(
                self.nc.sbuf_tensor("workar", [128, WORK_COLS], BF16))
        return self._work[0:nrow, c0:c0 + ncols]

    def sm(self, c0, ncols, nrow=128, f32=False):
        if self._sm is None:
            self._sm = self.stack.enter_context(
                self.nc.sbuf_tensor("smar", [128, SM_COLS], BF16))
        v = self._sm[0:nrow, c0:c0 + ncols]
        if f32:
            v = v.bitcast(F32)
        return v



def register_consts(p, b):
    cz = b.sb("constz", [128, 2], F32)
    s_gp = p.sem("s_gp")
    p.op("gp", "memset", cz[:, 0:1], 0.0, inc=s_gp)
    p.op("gp", "memset", cz[:, 1:2], 1e-5, inc=s_gp)
    b.nc.const_aps.register(F32, 0.0, cz[:, 0:1])
    b.nc.const_aps.register(F32, 1e-5, cz[:, 1:2])


def alloc_common(p, b):
    for i in range(4):
        b.ps(f"ps{i}", [128, 512], F32)
    b.ps("ps4", [128, 512], F32)
    b.ps("ps5", [128, 512], F32)
    b.ps("ptrA", [128, 1024], BF16)
    b.ps("ptrB", [128, 1024], BF16)


def emit_consts(p, b, consts):
    ones1 = b.sb("ones1", [1, 512], BF16)
    ones128 = b.sb("ones128", [128, 1], BF16)
    idb = b.sb("idb", [128, 128], BF16)
    idf = b.sb("idf", [128, 128], F32)
    trib = b.sb("trib", [128, 128], BF16)
    s_gp, s_ld = p.sem("s_gp"), p.sem("s_ldc")
    register_consts(p, b)
    p.op("gp", "memset", ones1[:], 1.0, inc=s_gp)
    p.op("gp", "memset", ones128[:], 1.0, inc=s_gp)
    p.dma("gp", idb[:], consts[:, 0:128], s_ld)
    p.dma("gp", trib[:], consts[:, 128:256], s_ld)
    p.dma("sp", idf[:], consts[:, 256:384], s_ld)
    return [(s_ld, s_ld.count), (s_gp, s_gp.count)]


def emit_load_x(p, b, x_ap):
    x_sb = b.sb("x_sb", [128, TB * D], BF16)
    s = p.sem("s_ldx")
    for tb in range(TB):
        p.dma("gp", x_sb[:, tb * D:(tb + 1) * D], x_ap[tb], s)
    return [(s, s.count)]


def emit_store_x(p, b, out_ap, waits):
    s = p.sem("s_stx")
    x_sb = b.t["x_sb"]
    for tb in range(TB):
        p.dma("gp", out_ap[tb], x_sb[:, tb * D:(tb + 1) * D], s, waits=waits)
    return [(s, s.count)]


def emit_ln(p, b, x_ready, consts_ready):
    x_sb = b.t["x_sb"]
    xnT = b.sb("xnT", [128, NDC * TOK], BF16)
    b.t["xnT"] = xnT
    xn_st = b.sm(0, 2 * D)
    bnst = b.sb("bnst", [128, TB * 24], F32)
    mv = b.sb("mv", [128, TB * 2], F32)
    stat = b.sb("stat", [128, 2 * TB], F32)
    ptrA, ptrB = b.t["ptrA"], b.t["ptrB"]
    s_dv, s_ac, s_pe = p.sem("s_dv"), p.sem("s_ac"), p.sem("s_pe")

    t_stats = None
    for tb in range(TB):
        for ch in range(4):
            t_stats = p.op("dv", "bn_stats",
                           bnst[:, tb * 24 + ch * 6: tb * 24 + (ch + 1) * 6],
                           x_sb[:, tb * D + ch * 512: tb * D + (ch + 1) * 512],
                           waits=x_ready,
                           inc=s_dv if (tb, ch) == (TB - 1, 3) else None)
    t_aggr = None
    for tb in range(TB):
        t_aggr = p.op("dv", "bn_aggr", mv[:, tb * 2: tb * 2 + 2],
                      bnst[:, tb * 24:(tb + 1) * 24],
                      waits=[(s_dv, t_stats)],
                      inc=s_dv if tb == TB - 1 else None)
    varv = mv[:].rearrange("p (t two) -> p t two", two=2)[:, :, 1:2]
    t_sqrt = p.op("ac", "activation", stat[:, 0:TB], varv, AF.Sqrt, bias=1e-5,
                  waits=[(s_dv, t_aggr)], inc=s_ac)
    t_rstd = p.op("dv", "reciprocal", stat[:, TB:2 * TB], stat[:, 0:TB],
                  waits=[(s_ac, t_sqrt)], inc=s_dv)

    tp_done, ev_done = {}, {}
    for tb in range(TB):
        par = tb % 2
        w = [(s_dv, t_rstd)] + x_ready
        if tb >= 2:
            w.append((s_pe, tp_done[(tb - 2, 1)]))
        t_app = p.op("dv", "tensor_scalar", xn_st[:, par * D:(par + 1) * D],
                     x_sb[:, tb * D:(tb + 1) * D],
                     mv[:, tb * 2: tb * 2 + 1], stat[:, TB + tb: TB + tb + 1],
                     ALU.subtract, ALU.mult, waits=w, inc=s_dv)
        for grp in range(2):
            bank = ptrA if grp == 0 else ptrB
            wts = [(s_dv, t_app)] + consts_ready
            if (tb - 1, grp) in ev_done:
                wts.append((s_dv, ev_done[(tb - 1, grp)]))
            tv = None
            for k in range(8):
                dc = grp * 8 + k
                tv = p.op("pe", "transpose", bank[:, k * 128:(k + 1) * 128],
                          xn_st[:, par * D + dc * 128: par * D + (dc + 1) * 128],
                          b.t["idb"][:], waits=wts,
                          inc=s_pe if k == 7 else None)
            tp_done[(tb, grp)] = tv
            ev_done[(tb, grp)] = p.op(
                "dv", "tensor_copy",
                xnT[:].rearrange("p (dc t) -> p dc t", dc=NDC)[
                    :, grp * 8:(grp + 1) * 8, tb * 128:(tb + 1) * 128],
                bank[:].rearrange("p (k t) -> p k t", k=8),
                waits=[(s_pe, tv)], inc=s_dv)
    return [(s_dv, p.sems["s_dv"].count)]


def xnT_mv(b, dc, t0, n):
    return b.t["xnT"][:, dc * TOK + t0: dc * TOK + t0 + n]


def xnT_st(b, dc, tb):
    return b.t["xnT"][:, dc * TOK + tb * 128: dc * TOK + (tb + 1) * 128]


def emit_gating(p, b, kind, tag, gate_wT_ap, gate_b_ap, xnT_ready,
                consts_ready):
    gwr = b.sm(0, D, nrow=8)
    lgT = b.sm(2048, 4096, nrow=8, f32=True)
    gwT = b.sb("gwT", [128, NDC * 8], BF16)
    gb = b.sb("gb", [1, 8], BF16)
    lg = b.sb("lg", [128, TB * 8], F32)
    top8 = b.sb("top8", [128, TB * 8], F32)
    wg = b.sb("wgate" + tag, [128, TB * 8], F32)
    ps_g, ps_t = b.t["ps4"], b.t["ps5"]
    ptrA = b.t["ptrA"]
    s_ld = p.sem("s_ldg" + tag)
    s_pe, s_dv, s_ac = p.sem("s_pe"), p.sem("s_dv"), p.sem("s_ac")

    p.dma("gp", gwr, gate_wT_ap, s_ld, waits=consts_ready + xnT_ready)
    if gate_b_ap is not None:
        p.dma("gp", gb[:], gate_b_ap, s_ld)
    v_w = s_ld.count

    for grp in range(2):
        tv = None
        for k in range(8):
            dc = grp * 8 + k
            tv = p.op("pe", "transpose", ptrA[:, k * 128: k * 128 + 8],
                      gwr[:, dc * 128:(dc + 1) * 128], b.t["idb"][0:8, 0:8],
                      waits=[(s_ld, v_w)] + xnT_ready +
                      ([(s_dv, p.sems["s_dv"].count)] if grp else []),
                      inc=s_pe if k == 7 else None)
        p.op("dv", "tensor_copy",
             gwT[:].rearrange("p (dc e) -> p dc e", e=8)[:, grp * 8:(grp + 1) * 8, :],
             ptrA[:].rearrange("p (k t) -> p k t", k=8)[:, :, 0:8],
             waits=[(s_pe, tv)], inc=s_dv)
    t_gwT = p.sems["s_dv"].count

    prev_ev = None
    for tc in range(4):
        def mmfn(t, tc=tc):
            last = None
            if gate_b_ap is not None:
                last = t.matmul(ps_g[0:8, 0:512], gb[0:1, :],
                                b.t["ones1"][0:1, 0:512], start=True, stop=False)
            for dc in range(NDC):
                last = t.matmul(ps_g[0:8, 0:512], gwT[:, dc * 8:(dc + 1) * 8],
                                xnT_mv(b, dc, tc * 512, 512),
                                start=(dc == 0 and gate_b_ap is None),
                                stop=(dc == NDC - 1))
            return last
        w = xnT_ready + [(s_dv, t_gwT)]
        if prev_ev is not None:
            w.append((s_ac, prev_ev))
        tv = p.op("pe", mmfn, waits=w, inc=s_pe)
        prev_ev = p.op("ac", "copy", lgT[:, tc * 512:(tc + 1) * 512],
                       ps_g[0:8, 0:512], waits=[(s_pe, tv)], inc=s_ac)

    tv = None
    for tb in range(TB):
        tv = p.op("pe", "transpose", ps_t[:, tb * 8: tb * 8 + 8],
                  lgT[:, tb * 128:(tb + 1) * 128], b.t["idf"][0:8, 0:8],
                  waits=[(s_ac, prev_ev)], inc=s_pe if tb == TB - 1 else None)
    t_lg = p.op("dv", "tensor_copy", lg[:], ps_t[:, 0:TB * 8],
                waits=[(s_pe, tv)], inc=s_dv)

    if kind == "ffn0":
        t_m = None
        for tb in range(TB):
            t_m = p.op("dv", "max", top8[:, tb * 8:(tb + 1) * 8],
                       lg[:, tb * 8:(tb + 1) * 8], waits=[(s_dv, t_lg)],
                       inc=s_dv if tb == TB - 1 else None)
        for tb in range(TB):
            p.op("dv", "tensor_scalar", wg[:, tb * 8:(tb + 1) * 8],
                 lg[:, tb * 8:(tb + 1) * 8],
                 top8[:, tb * 8 + 1: tb * 8 + 2], 0.25, ALU.is_ge, ALU.mult,
                 waits=[(s_dv, t_m)], inc=s_dv if tb == TB - 1 else None)
    else:
        nm = b.sb("g1nm", [128, TB], F32)
        ex = b.sb("g1ex", [128, TB * 8], F32)
        lsum = b.sb("g1ls", [128, TB], F32)
        rl = b.sb("g1rl", [128, TB], F32)
        pv = b.sb("g1pv", [128, TB * 8], F32)
        den = b.sb("g1den", [128, TB], F32)
        t1 = None
        for tb in range(TB):
            t1 = p.op("dv", "tensor_reduce", nm[:, tb: tb + 1],
                      lg[:, tb * 8:(tb + 1) * 8], AX.X, ALU.max, negate=True,
                      waits=[(s_dv, t_lg)], inc=s_dv if tb == TB - 1 else None)
        t2 = None
        for tb in range(TB):
            t2 = p.op("ac", "activation", ex[:, tb * 8:(tb + 1) * 8],
                      lg[:, tb * 8:(tb + 1) * 8], AF.Exp,
                      bias=nm[:, tb: tb + 1], accum_out=lsum[:, tb: tb + 1],
                      waits=[(s_dv, t1)], inc=s_ac if tb == TB - 1 else None)
        t3 = p.op("dv", "reciprocal", rl[:], lsum[:], waits=[(s_ac, t2)],
                  inc=s_dv)
        t4 = None
        for tb in range(TB):
            t4 = p.op("dv", "tensor_scalar_mul", pv[:, tb * 8:(tb + 1) * 8],
                      ex[:, tb * 8:(tb + 1) * 8], rl[:, tb: tb + 1],
                      waits=[(s_dv, t3)], inc=s_dv if tb == TB - 1 else None)
        t5 = None
        for tb in range(TB):
            t5 = p.op("dv", "max", top8[:, tb * 8:(tb + 1) * 8],
                      pv[:, tb * 8:(tb + 1) * 8], waits=[(s_dv, t4)],
                      inc=s_dv if tb == TB - 1 else None)
        t6 = None
        for tb in range(TB):
            t6 = p.op("dv", "tensor_scalar", den[:, tb: tb + 1],
                      top8[:, tb * 8: tb * 8 + 1],
                      top8[:, tb * 8 + 1: tb * 8 + 2], 1e-20,
                      ALU.add, ALU.add, waits=[(s_dv, t5)],
                      inc=s_dv if tb == TB - 1 else None)
        t7 = p.op("dv", "reciprocal", den[:], den[:], waits=[(s_dv, t6)],
                  inc=s_dv)
        t7b = p.op("dv", "tensor_scalar_mul", den[:], den[:], 0.5,
                   waits=[(s_dv, t7)], inc=s_dv)
        t8 = None
        for tb in range(TB):
            t8 = p.op("dv", "tensor_scalar", wg[:, tb * 8:(tb + 1) * 8],
                      pv[:, tb * 8:(tb + 1) * 8],
                      top8[:, tb * 8 + 1: tb * 8 + 2], None, ALU.is_ge,
                      waits=[(s_dv, t7b)], inc=s_dv if tb == TB - 1 else None)
        for tb in range(TB):
            p.op("dv", "scalar_tensor_tensor", wg[:, tb * 8:(tb + 1) * 8],
                 pv[:, tb * 8:(tb + 1) * 8], den[:, tb: tb + 1],
                 wg[:, tb * 8:(tb + 1) * 8], ALU.mult, ALU.mult,
                 waits=[(s_dv, t8)], inc=s_dv if tb == TB - 1 else None)
    return wg, [(s_dv, p.sems["s_dv"].count)]


def emit_ffn0(p, b, tag, expert_w_ap, expert_b_ap, wg, gate_ready, xnT_ready):
    x_sb = b.t["x_sb"]
    wbuf = [b.work(0, NDC * 512), b.work(NDC * 512, NDC * 512)]
    be = b.sm(0, 2 * D, nrow=1)[0:1, 0:D] if False else b.sm(0, D, nrow=1)
    tmp = [b.sb(f"f0t{i}", [128, 512], BF16) for i in range(2)]
    ps = [b.t[f"ps{i}"] for i in range(4)]
    s_w = [p.sem(f"s_f0w{tag}0"), p.sem(f"s_f0w{tag}1")]
    s_b = p.sem(f"s_f0b{tag}")
    s_pe, s_dv, s_ac = p.sem("s_pe"), p.sem("s_dv"), p.sem("s_ac")

    mm_done, ev_done = {}, {}
    idx = 0
    for e in range(E0):
        bw = [(s_pe, mm_done[(e - 1, 3)])] if e else (gate_ready + xnT_ready)
        bd = p.dma("gp", be, expert_b_ap[e: e + 1, :], s_b, waits=bw)
        for oc in range(4):
            par = (e * 4 + oc) % 2
            prev = (e, oc - 2) if oc >= 2 else (e - 1, oc + 2)
            w = [(s_b, bd)]
            if prev in mm_done:
                w.append((s_pe, mm_done[prev]))
            wd = p.dma("gp", wbuf[par].rearrange("p (dc o) -> p dc o", o=512),
                       expert_w_ap[e][:, oc * 512:(oc + 1) * 512]
                       .rearrange("(dc p) o -> p dc o", p=128),
                       s_w[par], waits=w)
            for tb in range(TB):
                bank = ps[idx % 4]
                w = xnT_ready + [(s_w[par], wd)]
                if idx - 4 in ev_done:
                    w.append((s_dv, ev_done[idx - 4]))

                def mmfn(t, oc=oc, par=par, bank=bank, tb=tb):
                    t.matmul(bank[:], b.t["ones1"][0:1, 0:128],
                             be[0:1, oc * 512:(oc + 1) * 512],
                             start=True, stop=False)
                    last = None
                    for dc in range(NDC):
                        last = t.matmul(bank[:], xnT_st(b, dc, tb),
                                        wbuf[par][:, dc * 512:(dc + 1) * 512],
                                        start=False, stop=(dc == NDC - 1))
                    return last
                tv = p.op("pe", mmfn, waits=w, inc=s_pe)
                if tb == TB - 1:
                    mm_done[(e, oc)] = tv
                slot = idx % 2
                w2 = [(s_pe, tv)] + gate_ready
                if idx - 2 in ev_done:
                    w2.append((s_dv, ev_done[idx - 2]))
                ta = p.op("ac", "activation", tmp[slot][:], bank[:], AF.Relu,
                          scale=wg[:, tb * 8 + e: tb * 8 + e + 1],
                          waits=w2, inc=s_ac)
                xs = x_sb[:, tb * D + oc * 512: tb * D + (oc + 1) * 512]
                ev_done[idx] = p.op("dv", "tensor_add", xs, xs, tmp[slot][:],
                                    waits=[(s_ac, ta)], inc=s_dv)
                idx += 1
    return [(s_dv, p.sems["s_dv"].count)]


def emit_qkv(p, b, tag, wqkv_ap, bqkv_ap, qkv_out_ap, xnT_ready):
    wbuf = [b.work(0, NDC * 512), b.work(NDC * 512, NDC * 512)]
    bq = b.sm(0, 3 * D, nrow=1)
    stg = [b.sb(f"qs{i}", [128, 512], BF16) for i in range(2)]
    ps = [b.t[f"ps{i}"] for i in range(4)]
    s_w = [p.sem(f"s_qw{tag}0"), p.sem(f"s_qw{tag}1")]
    s_st = [p.sem(f"s_qst{tag}0"), p.sem(f"s_qst{tag}1")]
    s_b = p.sem(f"s_qb{tag}")
    s_pe, s_dv, s_ac = p.sem("s_pe"), p.sem("s_dv"), p.sem("s_ac")

    bd = p.dma("gp", bq, bqkv_ap, s_b, waits=xnT_ready)
    mm_done, ev_done = {}, {}
    idx = 0
    for oc in range(12):
        par = oc % 2
        w = [(s_b, bd)]
        if oc - 2 in mm_done:
            w.append((s_pe, mm_done[oc - 2]))
        wd = p.dma("gp", wbuf[par].rearrange("p (dc o) -> p dc o", o=512),
                   wqkv_ap[:, oc * 512:(oc + 1) * 512]
                   .rearrange("(dc p) o -> p dc o", p=128), s_w[par], waits=w)
        for tb in range(TB):
            bank = ps[idx % 4]
            w = xnT_ready + [(s_w[par], wd)]
            if idx - 4 in ev_done:
                eng4, cnt4, _ = ev_done[idx - 4]
                w.append((eng4, cnt4))

            def mmfn(t, oc=oc, par=par, bank=bank, tb=tb):
                t.matmul(bank[:], b.t["ones1"][0:1, 0:128],
                         bq[0:1, oc * 512:(oc + 1) * 512],
                         start=True, stop=False)
                last = None
                for dc in range(NDC):
                    last = t.matmul(bank[:], xnT_st(b, dc, tb),
                                    wbuf[par][:, dc * 512:(dc + 1) * 512],
                                    start=False, stop=(dc == NDC - 1))
                return last
            tv = p.op("pe", mmfn, waits=w, inc=s_pe)
            if tb == TB - 1:
                mm_done[oc] = tv
            slot = idx % 2
            w2 = [(s_pe, tv)]
            if idx - 2 in ev_done:
                w2.append((s_st[slot], ev_done[idx - 2][2]))
            if idx % 2 == 0:
                sE = s_ac
                ev = p.op("ac", "copy", stg[slot][:], bank[:], waits=w2, inc=sE)
            else:
                sE = s_dv
                ev = p.op("dv", "tensor_copy", stg[slot][:], bank[:],
                          waits=w2, inc=sE)
            st = p.dma("sp", qkv_out_ap[tb * 128:(tb + 1) * 128,
                                        oc * 512:(oc + 1) * 512],
                       stg[slot][:], s_st[slot], waits=[(sE, ev)])
            ev_done[idx] = (sE, ev, st)
            idx += 1
    return [(s_st[0], p.sems[f"s_qst{tag}0"].count),
            (s_st[1], p.sems[f"s_qst{tag}1"].count)]


def emit_rope_norms(p, b, qkv_ap, ct_ap, st_ap, norms_out_ap, qkv_stored,
                    layout, consts_ready):
    l0 = layout == "l0"
    qk = b.work(0, 4096)
    sq = b.work(4096, 4096)
    acc = b.sb("racc", [128, 32], F32)
    red = b.sb("rred", [128, 32], F32)
    accT = b.sb("raccT", [32, 128], F32)
    nmax = b.sb("rnmax", [32, 1], F32)
    nout = b.sb("rnout", [1, 32], F32)
    ps_n = b.t["ps5"]
    s_ld, s_st = p.sem("s_rld"), p.sem("s_rst")
    s_dv, s_ac, s_pe = p.sem("s_dv"), p.sem("s_ac"), p.sem("s_pe")
    if l0:
        ct = b.sm(0, 512, f32=True)
        stt = b.sm(512, 512, f32=True)
        sc = [b.sm(1024 + i * 512, 512, f32=True) for i in range(4)]

    qk3 = qk.rearrange("p (h c) -> p h c", c=256)

    def qk_src(tb):
        if l0:
            return qkv_ap[tb * 128:(tb + 1) * 128, :] \
                .rearrange("p (h c) -> p h c", c=384)[:, :, 0:256]
        return qkv_ap[tb * 128:(tb + 1) * 128, 0:4096]

    prev_store = None
    prev_sq = None
    prev_red = None
    for tb in range(TB):
        w = list(qkv_stored)
        if prev_store is not None:
            w.append((s_st, prev_store))
        if prev_sq is not None:
            w.append((s_ac, prev_sq))
        ld = p.dma("sp", qk3 if l0 else qk, qk_src(tb), s_ld, waits=w)
        if l0:
            p.dma("sp", ct, ct_ap[tb * 128:(tb + 1) * 128, :], s_ld)
            ld = p.dma("sp", stt, st_ap[tb * 128:(tb + 1) * 128, :], s_ld)
        ldv = [(s_ld, ld)]
        t_rot = None
        if l0:
            c0 = ct.rearrange("p (h j) -> p h j", j=16)
            s0 = stt.rearrange("p (h j) -> p h j", j=16)
            views = [s.rearrange("p (h j) -> p h j", j=16) for s in sc]
            for off in (0, 128):
                v0 = qk3[:, :, off + 0: off + 16]
                v1 = qk3[:, :, off + 16: off + 32]
                w0 = ldv + ([(s_dv, t_rot)] if t_rot else [])
                p.op("dv", "tensor_mul", views[0], v0, c0, waits=w0)
                p.op("dv", "tensor_mul", views[1], v1, s0, waits=w0)
                p.op("dv", "tensor_mul", views[2], v1, c0, waits=w0)
                t4 = p.op("dv", "tensor_mul", views[3], v0, s0, waits=w0,
                          inc=p.sem("s_dv"))
                p.op("dv", "tensor_sub", v0, views[0], views[1],
                     waits=[(s_dv, t4)])
                t_rot = p.op("dv", "tensor_add", v1, views[2], views[3],
                             waits=[(s_dv, t4)], inc=s_dv)
        rw = ldv + ([(s_dv, t_rot)] if t_rot else [])
        prev_sq = p.op("ac", "activation", sq, qk, AF.Square,
                       waits=rw, inc=s_ac)
        if l0:
            sqv = sq.rearrange("p (h g c) -> p g h c", g=2, c=128)
        else:
            sqv = sq.rearrange("p (g h c) -> p g h c", g=2, c=128)
        tr = p.op("dv", "tensor_reduce",
                  red[:].rearrange("p (g h) -> p g h", g=2), sqv, AX.X, ALU.add,
                  waits=[(s_ac, prev_sq)], inc=s_dv)
        if tb == 0:
            prev_red = p.op("dv", "tensor_copy", acc[:], red[:],
                            waits=[(s_dv, tr)], inc=s_dv)
        else:
            prev_red = p.op("dv", "tensor_max", acc[:], acc[:], red[:],
                            waits=[(s_dv, tr), (s_dv, prev_red)], inc=s_dv)
        if l0:
            prev_store = p.dma("sp", qk_src(tb), qk3, s_st,
                               waits=[(s_dv, t_rot), (s_ac, prev_sq)])

    tv = p.op("pe", "transpose", ps_n[0:32, 0:128], acc[:], b.t["idf"][:],
              waits=[(s_dv, prev_red)] + consts_ready, inc=s_pe)
    e1 = p.op("dv", "tensor_copy", accT[:], ps_n[0:32, 0:128],
              waits=[(s_pe, tv)], inc=s_dv)
    r2 = p.op("dv", "tensor_reduce", nmax[:], accT[:], AX.X, ALU.max,
              waits=[(s_dv, e1)], inc=s_dv)
    tv2 = p.op("pe", "transpose", ps_n[0:1, 0:32], nmax[:], b.t["idf"][0:32, 0:32],
               waits=[(s_dv, r2)], inc=s_pe)
    e2 = p.op("dv", "tensor_copy", nout[:], ps_n[0:1, 0:32],
              waits=[(s_pe, tv2)], inc=s_dv)
    stn = p.dma("sp", norms_out_ap, nout[:], s_st, waits=[(s_dv, e2)])
    return [(s_st, stn)]


def emit_ffn1(p, b, tag, experts, xnT_ready):
    x_sb = b.t["x_sb"]
    gbuf = [b.work(i * 2048, 2048) for i in range(2)]
    ubuf = [b.work(4096 + i * 2048, 2048) for i in range(2)]
    dbuf = [b.work(8192 + i * 2048, 2048) for i in range(4)]
    hsb = b.work(16384, 4 * 2048)
    sil = [b.sb(f"f1s{i}", [128, 512], BF16) for i in range(2)]
    ps = [b.t[f"ps{i}"] for i in range(4)]
    s_g = [p.sem(f"s_f1g{tag}0"), p.sem(f"s_f1g{tag}1")]
    s_u = [p.sem(f"s_f1u{tag}0"), p.sem(f"s_f1u{tag}1")]
    s_d = [p.sem(f"s_f1d{tag}{i}") for i in range(4)]
    s_pe, s_dv, s_ac = p.sem("s_pe"), p.sem("s_dv"), p.sem("s_ac")

    groups = [list(range(0, 4)), list(range(4, 8)), list(range(8, NHC1))]
    prev_down_mm = None
    down_ev = None
    comb = {}
    gu_last = {}
    for ei, (wg_ap, wu_ap, wd_ap, gate) in enumerate(experts):
        for gi, grp in enumerate(groups):
            nhc = len(grp)
            comb.clear()
            gu_last.clear()
            for li, hc in enumerate(grp):
                par = li % 2
                w = list(xnT_ready)
                if down_ev is not None:
                    w.append((s_dv, down_ev))
                if prev_down_mm is not None:
                    w.append((s_pe, prev_down_mm))
                if (par, "last") in gu_last:
                    w.append((s_dv, gu_last[(par, "last")]))
                gd = p.dma("gp", gbuf[par].rearrange("p (dc h) -> p dc h", h=128),
                           wg_ap[:, hc * 128:(hc + 1) * 128]
                           .rearrange("(dc p) h -> p dc h", p=128),
                           s_g[par], waits=w)
                ud = p.dma("gp", ubuf[par].rearrange("p (dc h) -> p dc h", h=128),
                           wu_ap[:, hc * 128:(hc + 1) * 128]
                           .rearrange("(dc p) h -> p dc h", p=128),
                           s_u[par], waits=w)
                for tc in range(4):
                    bA = ps[(tc % 2) * 2]
                    bU = ps[(tc % 2) * 2 + 1]
                    w2 = xnT_ready + [(s_g[par], gd), (s_u[par], ud)]
                    guard = (li, tc - 2) if tc >= 2 else (li - 1, tc + 2)
                    if guard in comb:
                        w2.append((s_dv, comb[guard]))
                    if down_ev is not None:
                        w2.append((s_dv, down_ev))
                    if prev_down_mm is not None:
                        w2.append((s_pe, prev_down_mm))

                    def gmm(t, par=par, tc=tc, bA=bA):
                        last = None
                        for dc in range(NDC):
                            last = t.matmul(
                                bA[:], gbuf[par][:, dc * 128:(dc + 1) * 128],
                                xnT_mv(b, dc, tc * 512, 512),
                                start=(dc == 0), stop=(dc == NDC - 1))
                        return last

                    def umm(t, par=par, tc=tc, bU=bU):
                        last = None
                        for dc in range(NDC):
                            last = t.matmul(
                                bU[:], ubuf[par][:, dc * 128:(dc + 1) * 128],
                                xnT_mv(b, dc, tc * 512, 512),
                                start=(dc == 0), stop=(dc == NDC - 1))
                        return last
                    tg = p.op("pe", gmm, waits=w2, inc=s_pe)
                    tu = p.op("pe", umm, waits=w2, inc=s_pe)
                    slot = tc % 2
                    w3 = [(s_pe, tg)]
                    if (li, tc - 2) in comb or (li - 1, tc + 2) in comb:
                        pass
                    prevc = comb.get((li, tc - 2), comb.get((li - 1, tc + 2)))
                    if prevc is not None:
                        w3.append((s_dv, prevc))
                    tsil = p.op("ac", "activation", sil[slot][:], bA[:],
                                AF.Silu, waits=w3, inc=s_ac)
                    cc = p.op("dv", "tensor_mul",
                              hsb[:, li * TOK + tc * 512: li * TOK + (tc + 1) * 512],
                              sil[slot][:], bU[:],
                              waits=[(s_pe, tu), (s_ac, tsil)], inc=s_dv)
                    comb[(li, tc)] = cc
                gu_last[(par, "last")] = comb[(li, 3)]
            comb_all = p.sems["s_dv"].count
            dld = []
            for li, hc in enumerate(grp):
                w = [(s_pe, prev_down_mm)] if prev_down_mm else []
                dld.append(p.dma("gp", dbuf[li],
                                 wd_ap[hc * 128:(hc + 1) * 128, :],
                                 s_d[li], waits=w))
            last_mm = None
            prev_ev = down_ev
            for tb in range(TB):
                w = [(s_dv, comb_all)] + \
                    [(s_d[li], dld[li]) for li in range(nhc)]
                if prev_ev is not None:
                    w.append((s_dv, prev_ev))

                def dmm(t, tb=tb, nhc=nhc):
                    last = None
                    for li in range(nhc):
                        for dcol in range(4):
                            last = t.matmul(
                                ps[dcol][:],
                                hsb[:, li * TOK + tb * 128: li * TOK + (tb + 1) * 128],
                                dbuf[li][:, dcol * 512:(dcol + 1) * 512],
                                start=(li == 0), stop=(li == nhc - 1))
                    return last
                last_mm = p.op("pe", dmm, waits=w, inc=s_pe)
                for dcol in range(4):
                    xs = x_sb[:, tb * D + dcol * 512: tb * D + (dcol + 1) * 512]
                    if gate[0] == "col":
                        sc_ = gate[1][:, tb * 8 + gate[2]: tb * 8 + gate[2] + 1]
                    else:
                        sc_ = gate[1]
                    prev_ev = p.op("dv", "scalar_tensor_tensor", xs,
                                   ps[dcol][:], sc_, xs, ALU.mult, ALU.add,
                                   waits=[(s_pe, last_mm)], inc=s_dv)
            prev_down_mm = last_mm
            down_ev = prev_ev
    return [(s_dv, p.sems["s_dv"].count)]


def emit_proj(p, b, tag, oT_ap, wdense_ap, bdense_ap, x_ready, consts_ready):
    x_sb = b.t["x_sb"]
    osb = b.work(0, 4 * TOK)
    dbf = [b.work(4 * TOK + i * D, D) for i in range(4)]
    bq = b.sm(0, D, nrow=1)
    ps = [b.t[f"ps{i}"] for i in range(4)]
    s_o = p.sem(f"s_pjo{tag}")
    s_wd = [p.sem(f"s_pjw{tag}{i}") for i in range(4)]
    s_b = p.sem(f"s_pjb{tag}")
    s_pe, s_dv = p.sem("s_pe"), p.sem("s_dv")

    bd = p.dma("gp", bq, bdense_ap, s_b, waits=consts_ready + x_ready)
    prev_mm = None
    prev_ev = None
    for hg in range(4):
        w = ([(s_pe, prev_mm)] if prev_mm else [])
        olast = None
        wlast = []
        for i in range(4):
            h = hg * 4 + i
            olast = p.dma("sp", osb[:, i * TOK:(i + 1) * TOK], oT_ap[h],
                          s_o, waits=w)
            wlast.append(p.dma("gp", dbf[i], wdense_ap[h * 128:(h + 1) * 128, :],
                               s_wd[i], waits=w))
        ow = [(s_o, olast)] + \
            [(s_wd[i], wlast[i]) for i in range(4)]
        for tb in range(TB):
            w2 = ow + [(s_b, bd)] + x_ready
            if prev_ev is not None:
                w2.append((s_dv, prev_ev))

            def mmfn(t, hg=hg, tb=tb):
                last = None
                for dcol in range(4):
                    if hg == 0:
                        t.matmul(ps[dcol][:], b.t["ones1"][0:1, 0:128],
                                 bq[0:1, dcol * 512:(dcol + 1) * 512],
                                 start=True, stop=False)
                for i in range(4):
                    for dcol in range(4):
                        last = t.matmul(
                            ps[dcol][:],
                            osb[:, i * TOK + tb * 128: i * TOK + (tb + 1) * 128],
                            dbf[i][:, dcol * 512:(dcol + 1) * 512],
                            start=(hg != 0 and i == 0), stop=(i == 3))
                return last
            prev_mm = p.op("pe", mmfn, waits=w2, inc=s_pe)
            for dcol in range(4):
                xs = x_sb[:, tb * D + dcol * 512: tb * D + (dcol + 1) * 512]
                prev_ev = p.op("dv", "tensor_add", xs, xs, ps[dcol][:],
                               waits=[(s_pe, prev_mm)], inc=s_dv)
    return [(s_dv, p.sems["s_dv"].count)]



def build_L1(upto="all"):
    nc = bass.Bass("TRN2", target_bir_lowering=False, debug=False)
    x_in = nc.dram_tensor("x_in", [TB, 128, D], F32, kind="ExternalInput").ap()
    gate_wT = nc.dram_tensor("gate_wT", [8, D], F32, kind="ExternalInput").ap()
    gate_b = nc.dram_tensor("gate_b", [1, 8], F32, kind="ExternalInput").ap()
    exp_w = nc.dram_tensor("exp_w", [E0, D, D], F32, kind="ExternalInput").ap()
    exp_b = nc.dram_tensor("exp_b", [E0, D], F32, kind="ExternalInput").ap()
    wqkv = nc.dram_tensor("wqkv", [D, 3 * D], F32, kind="ExternalInput").ap()
    bqkv = nc.dram_tensor("bqkv", [1, 3 * D], F32, kind="ExternalInput").ap()
    ct_t = nc.dram_tensor("ct", [TOK, 256], F32, kind="ExternalInput").ap()
    st_t = nc.dram_tensor("st", [TOK, 256], F32, kind="ExternalInput").ap()
    consts = nc.dram_tensor("consts", [128, 384], F32, kind="ExternalInput").ap()
    x1 = nc.dram_tensor("x1", [TB, 128, D], F32, kind="ExternalOutput").ap()
    qkv = nc.dram_tensor("qkv", [TOK, 3 * D], BF16, kind="ExternalOutput").ap()
    norms = nc.dram_tensor("norms", [1, 32], F32, kind="ExternalOutput").ap()

    with ExitStack() as stack:
        p, b = Prog(nc), Bufs(nc, stack)
        alloc_common(p, b)
        cr = emit_consts(p, b, consts)
        xr = emit_load_x(p, b, x_in)
        fin = []
        if upto == "load":
            stx = emit_store_x(p, b, x1, xr)
            p.finalize(stack, stx)
            return nc
        ln1 = emit_ln(p, b, xr, cr)
        if upto == "ln":
            stx = emit_store_x(p, b, x1, ln1)
            p.finalize(stack, stx)
            return nc
        wg, gr = emit_gating(p, b, "ffn0", "a", gate_wT, gate_b, ln1, cr)
        if upto == "gate":
            stx = emit_store_x(p, b, x1, gr)
            p.finalize(stack, stx)
            return nc
        f0 = emit_ffn0(p, b, "a", exp_w, exp_b, wg, gr, ln1)
        stx = emit_store_x(p, b, x1, f0)
        if upto == "ffn0":
            p.finalize(stack, stx)
            return nc
        ln2 = emit_ln(p, b, f0, cr)
        qkd = emit_qkv(p, b, "a", wqkv, bqkv, qkv, ln2)
        if upto == "qkv":
            p.finalize(stack, stx + qkd)
            return nc
        rn = emit_rope_norms(p, b, qkv, ct_t, st_t, norms, qkd, "l0", cr)
        p.finalize(stack, stx + rn + qkd)
    return nc


def build_L3(upto="all", nexperts=10):
    nc = bass.Bass("TRN2", target_bir_lowering=False, debug=False)
    x_in = nc.dram_tensor("x_in", [TB, 128, D], F32, kind="ExternalInput").ap()
    oT = nc.dram_tensor("oT", [NH, 128, TOK], BF16, kind="ExternalInput").ap()
    wdense = nc.dram_tensor("wdense", [D, D], F32, kind="ExternalInput").ap()
    bdense = nc.dram_tensor("bdense", [1, D], F32, kind="ExternalInput").ap()
    gate_wT = nc.dram_tensor("gate_wT", [8, D], F32, kind="ExternalInput").ap()
    gate_b = nc.dram_tensor("gate_b", [1, 8], F32, kind="ExternalInput").ap()
    exp_w = nc.dram_tensor("exp_w", [E0, D, D], F32, kind="ExternalInput").ap()
    exp_b = nc.dram_tensor("exp_b", [E0, D], F32, kind="ExternalInput").ap()
    rout_w = nc.dram_tensor("rout_w", [8, D], F32, kind="ExternalInput").ap()
    f1wg = nc.dram_tensor("f1wg", [E0, D, H1], F32, kind="ExternalInput").ap()
    f1wu = nc.dram_tensor("f1wu", [E0, D, H1], F32, kind="ExternalInput").ap()
    f1wd = nc.dram_tensor("f1wd", [E0, H1, D], F32, kind="ExternalInput").ap()
    f1sg = nc.dram_tensor("f1sg", [D, HS], F32, kind="ExternalInput").ap()
    f1su = nc.dram_tensor("f1su", [D, HS], F32, kind="ExternalInput").ap()
    f1sd = nc.dram_tensor("f1sd", [HS, D], F32, kind="ExternalInput").ap()
    wqkv = nc.dram_tensor("wqkv", [D, 3 * D], F32, kind="ExternalInput").ap()
    bqkv = nc.dram_tensor("bqkv", [1, 3 * D], F32, kind="ExternalInput").ap()
    consts = nc.dram_tensor("consts", [128, 384], F32, kind="ExternalInput").ap()
    x4 = nc.dram_tensor("x4", [TB, 128, D], F32, kind="ExternalOutput").ap()
    qkv = nc.dram_tensor("qkv", [TOK, 3 * D], BF16, kind="ExternalOutput").ap()
    norms = nc.dram_tensor("norms", [1, 32], F32, kind="ExternalOutput").ap()

    with ExitStack() as stack:
        p, b = Prog(nc), Bufs(nc, stack)
        alloc_common(p, b)
        cr = emit_consts(p, b, consts)
        xr = emit_load_x(p, b, x_in)
        pj = emit_proj(p, b, "a", oT, wdense, bdense, xr, cr)
        if upto == "proj":
            stx = emit_store_x(p, b, x4, pj)
            p.finalize(stack, stx)
            return nc
        ln1 = emit_ln(p, b, pj, cr)
        wg0, gr0 = emit_gating(p, b, "ffn0", "a", gate_wT, gate_b, ln1, cr)
        f0 = emit_ffn0(p, b, "a", exp_w, exp_b, wg0, gr0, ln1)
        if upto == "ffn0":
            stx = emit_store_x(p, b, x4, f0)
            p.finalize(stack, stx)
            return nc
        ln2 = emit_ln(p, b, f0, cr)
        wg1, gr1 = emit_gating(p, b, "ffn1", "b", rout_w, None, ln2, cr)
        experts = [(f1wg[e], f1wu[e], f1wd[e], ("col", wg1, e))
                   for e in range(E0)]
        experts.append((f1sg[:, 0:H1], f1su[:, 0:H1], f1sd[0:H1, :],
                        ("const", 0.5)))
        experts.append((f1sg[:, H1:HS], f1su[:, H1:HS], f1sd[H1:HS, :],
                        ("const", 0.5)))
        f1 = emit_ffn1(p, b, "a", experts[:nexperts], ln2 + gr1)
        stx = emit_store_x(p, b, x4, f1)
        if upto == "ffn1":
            p.finalize(stack, stx)
            return nc
        ln3 = emit_ln(p, b, f1, cr)
        qkd = emit_qkv(p, b, "b", wqkv, bqkv, qkv, ln3)
        rn = emit_rope_norms(p, b, qkv, None, None, norms, qkd, "l1", cr)
        p.finalize(stack, stx + rn + qkd)
    return nc


def build_L5():
    nc = bass.Bass("TRN2", target_bir_lowering=False, debug=False)
    x_in = nc.dram_tensor("x_in", [TB, 128, D], F32, kind="ExternalInput").ap()
    oT = nc.dram_tensor("oT", [NH, 128, TOK], BF16, kind="ExternalInput").ap()
    wdense = nc.dram_tensor("wdense", [D, D], F32, kind="ExternalInput").ap()
    bdense = nc.dram_tensor("bdense", [1, D], F32, kind="ExternalInput").ap()
    rout_w = nc.dram_tensor("rout_w", [8, D], F32, kind="ExternalInput").ap()
    f1wg = nc.dram_tensor("f1wg", [E0, D, H1], F32, kind="ExternalInput").ap()
    f1wu = nc.dram_tensor("f1wu", [E0, D, H1], F32, kind="ExternalInput").ap()
    f1wd = nc.dram_tensor("f1wd", [E0, H1, D], F32, kind="ExternalInput").ap()
    f1sg = nc.dram_tensor("f1sg", [D, HS], F32, kind="ExternalInput").ap()
    f1su = nc.dram_tensor("f1su", [D, HS], F32, kind="ExternalInput").ap()
    f1sd = nc.dram_tensor("f1sd", [HS, D], F32, kind="ExternalInput").ap()
    consts = nc.dram_tensor("consts", [128, 384], F32, kind="ExternalInput").ap()
    x_out = nc.dram_tensor("x_out", [TB, 128, D], F32, kind="ExternalOutput").ap()

    with ExitStack() as stack:
        p, b = Prog(nc), Bufs(nc, stack)
        alloc_common(p, b)
        cr = emit_consts(p, b, consts)
        xr = emit_load_x(p, b, x_in)
        pj = emit_proj(p, b, "a", oT, wdense, bdense, xr, cr)
        ln1 = emit_ln(p, b, pj, cr)
        wg1, gr1 = emit_gating(p, b, "ffn1", "b", rout_w, None, ln1, cr)
        experts = [(f1wg[e], f1wu[e], f1wd[e], ("col", wg1, e))
                   for e in range(E0)]
        experts.append((f1sg[:, 0:H1], f1su[:, 0:H1], f1sd[0:H1, :],
                        ("const", 0.5)))
        experts.append((f1sg[:, H1:HS], f1su[:, H1:HS], f1sd[H1:HS, :],
                        ("const", 0.5)))
        f1 = emit_ffn1(p, b, "a", experts, ln1 + gr1)
        stx = emit_store_x(p, b, x_out, f1)
        p.finalize(stack, stx)
    return nc


def build_att(causal):
    nc = bass.Bass("TRN2", target_bir_lowering=False, debug=False)
    qT = nc.dram_tensor("qT", [4, 128, SEQ], BF16, kind="ExternalInput").ap()
    kT = nc.dram_tensor("kT", [4, 128, SEQ], BF16, kind="ExternalInput").ap()
    vv = nc.dram_tensor("vv", [4, SEQ, HD], BF16, kind="ExternalInput").ap()
    nrm = nc.dram_tensor("nrm", [1, 32], F32, kind="ExternalInput").ap()
    consts = nc.dram_tensor("consts", [128, 384], F32, kind="ExternalInput").ap()
    oT = nc.dram_tensor("oT", [4, 128, SEQ], BF16, kind="ExternalOutput").ap()

    with ExitStack() as stack:
        p, b = Prog(nc), Bufs(nc, stack)
        ps = [b.ps(f"ps{i}", [128, 512], F32) for i in range(4)]
        psb = b.ps("psb", [128, 512], F32)
        qsb = [b.sb(f"aq{i}", [128, SEQ], BF16) for i in range(2)]
        ksb = [b.sb(f"ak{i}", [128, SEQ], BF16) for i in range(2)]
        vsb = [b.sb(f"av{i}", [128, SEQ], BF16) for i in range(2)]
        psl = [b.sb(f"apl{i}", [128, 512], BF16) for i in range(2)]
        ost = [b.sb(f"ao{i}", [128, 512], BF16) for i in range(2)]
        lsb = b.sb("al", [1, 512], F32)
        rlb = b.sb("arl", [1, 512], BF16)
        rbc = [b.sb(f"arb{i}", [128, 512], BF16) for i in range(2)]
        nin = b.sb("anin", [1, 32], F32)
        n8 = b.sb("an8", [1, 8], F32)
        prod = b.sb("aprod", [1, 4], F32)
        negm = b.sb("anegm", [1, 4], F32)
        negmb = b.sb("anegmb", [1, 4], BF16)
        negM = b.sb("aM", [128, 4], F32)
        ones1 = b.sb("ones1", [1, 512], BF16)
        ones128 = b.sb("ones128", [128, 1], BF16)
        trib = b.sb("trib", [128, 128], BF16)
        s_gp, s_ldc = p.sem("s_gp"), p.sem("s_ldc")
        register_consts(p, b)
        s_ld = [p.sem("s_ald0"), p.sem("s_ald1")]
        s_st = [p.sem("s_ast0"), p.sem("s_ast1")]
        s_pe, s_dv, s_ac = p.sem("s_pe"), p.sem("s_dv"), p.sem("s_ac")

        p.op("gp", "memset", ones1[:], 1.0, inc=s_gp)
        p.op("gp", "memset", ones128[:], 1.0, inc=s_gp)
        p.dma("gp", trib[:], consts[:, 128:256], s_ldc)
        p.dma("sp", nin[:], nrm, s_ldc)
        cw = [(s_ldc, 32), (s_gp, p.sems["s_gp"].count)]
        t1 = p.op("dv", "tensor_reduce",
                  n8[:].rearrange("p (u g) -> p u g", g=2),
                  nin[:].rearrange("p (u c g) -> p u g c", c=4, g=2),
                  AX.X, ALU.max, waits=cw, inc=s_dv)
        t2 = p.op("ac", "activation", n8[:], n8[:], AF.Sqrt,
                  waits=[(s_dv, t1)], inc=s_ac)
        n83 = n8[:].rearrange("p (u g) -> p u g", g=2)
        t3 = p.op("dv", "tensor_mul",
                  prod[:].rearrange("p (u g) -> p u g", g=1),
                  n83[:, :, 0:1], n83[:, :, 1:2], waits=[(s_ac, t2)], inc=s_dv)
        t4 = p.op("dv", "tensor_scalar", negm[:], prod[:], -1.05 * SCALE, -1.0,
                  ALU.mult, ALU.add, waits=[(s_dv, t3)], inc=s_dv)
        t5 = p.op("dv", "tensor_copy", negmb[:], negm[:], waits=[(s_dv, t4)],
                  inc=s_dv)
        t6 = p.op("pe", lambda t: t.matmul(psb[:, 0:4], ones1[0:1, 0:128],
                                           negmb[0:1, :], start=True, stop=True),
                  waits=[(s_dv, t5), (s_gp, 4)], inc=s_pe)
        t7 = p.op("dv", "tensor_copy", negM[:], psb[:, 0:4],
                  waits=[(s_pe, t6)], inc=s_dv)
        m_ready = [(s_dv, t7)]

        unit_mm_done = {}
        for u in range(4):
            par = u % 2
            w = [] if u < 2 else [(s_pe, unit_mm_done[u - 2])]
            p.dma("sp", qsb[par][:], qT[u], s_ld[par], waits=w)
            p.dma("sp", ksb[par][:], kT[u], s_ld[par])
            l3 = p.dma("sp", vsb[par][:].rearrange("p (n m) -> p n m", m=HD),
                       vv[u].rearrange("(n p) m -> p n m", p=128),
                       s_ld[par])
            ldw = [(s_ld[par], l3)]
            last_unit_mm = None
            prev_exp = {}
            for qc in range(16):
                nkt = 4 * qc + 4 if causal else GT
                first = True
                for j in range(nkt):
                    gk = j
                    i0 = max(0, gk - 4 * qc) if causal else 0
                    off = i0 * 128
                    sl = j % 2
                    sbank = ps[sl]
                    w2 = ldw + m_ready
                    if (qc, j - 2) in prev_exp:
                        w2.append((s_ac, prev_exp[(qc, j - 2)]))

                    def smm(t, sbank=sbank, gk=gk, off=off, par=par, qc=qc):
                        return t.matmul(
                            sbank[:, off:512],
                            ksb[par][:, gk * 128:(gk + 1) * 128],
                            qsb[par][:, qc * 512 + off:(qc + 1) * 512],
                            start=True, stop=True)
                    ts = p.op("pe", smm, waits=w2, inc=s_pe)
                    te = p.op("ac", "activation", psl[sl][:, off:512],
                              sbank[:, off:512], AF.Exp, scale=SCALE,
                              bias=negM[:, u: u + 1], waits=[(s_pe, ts)],
                              inc=s_ac)
                    prev_exp[(qc, j)] = te
                    wpv = [(s_ac, te)]
                    if causal and 0 <= gk - 4 * qc < 4:
                        tt = p.op("dv", "tensor_mul",
                                  psl[sl][:, off: off + 128],
                                  psl[sl][:, off: off + 128], trib[:],
                                  waits=[(s_ac, te)] + cw, inc=s_dv)
                        wpv = [(s_dv, tt)]
                    last = (j == nkt - 1)

                    def lmm(t, sl=sl, off=off, first=first, last=last):
                        return t.matmul(ps[3][0:1, off:512], ones128[:],
                                        psl[sl][:, off:512],
                                        start=first, stop=last)

                    def pvmm(t, sl=sl, off=off, gk=gk, par=par, first=first,
                             last=last):
                        return t.matmul(ps[2][:, off:512],
                                        vsb[par][:, gk * 128:(gk + 1) * 128],
                                        psl[sl][:, off:512],
                                        start=first, stop=last)
                    p.op("pe", lmm, waits=wpv, inc=s_pe)
                    last_unit_mm = p.op("pe", pvmm, waits=wpv, inc=s_pe)
                    first = False
                el = p.op("dv", "tensor_copy", lsb[:], ps[3][0:1, 0:512],
                          waits=[(s_pe, last_unit_mm)], inc=s_dv)
                er = p.op("dv", "reciprocal", lsb[:], lsb[:],
                          waits=[(s_dv, el)], inc=s_dv)
                eb = p.op("dv", "tensor_copy", rlb[:], lsb[:],
                          waits=[(s_dv, er)], inc=s_dv)
                tb6 = p.op("pe", lambda t: t.matmul(
                    psb[:, 0:512], ones1[0:1, 0:128], rlb[0:1, :],
                    start=True, stop=True), waits=[(s_dv, eb)], inc=s_pe)
                rslot = qc % 2
                ebc = p.op("dv", "tensor_copy", rbc[rslot][:], psb[:, 0:512],
                           waits=[(s_pe, tb6)], inc=s_dv)
                oslot = qc % 2
                w3 = [(s_dv, ebc)]
                if qc >= 2:
                    w3.append((s_st[oslot],
                               p.sems[f"s_ast{oslot}"].count))
                eo = p.op("dv", "scalar_tensor_tensor", ost[oslot][:],
                          ps[2][:, 0:512], 1.0, rbc[rslot][:],
                          ALU.mult, ALU.mult, waits=w3, inc=s_dv)
                p.dma("sp", oT[u][:, qc * 512:(qc + 1) * 512], ost[oslot][:],
                      s_st[oslot], waits=[(s_dv, eo)])
                ldw = ldw + [(s_dv, eo)]
            unit_mm_done[u] = last_unit_mm
        p.finalize(stack, [(s_st[0], p.sems["s_ast0"].count),
                           (s_st[1], p.sems["s_ast1"].count)])
    return nc



def _consts_host():
    c = np.zeros((128, 384), np.float32)
    c[:, 0:128] = np.eye(128)
    kk, qq = np.mgrid[0:128, 0:128]
    c[:, 128:256] = (kk <= qq).astype(np.float32)
    c[:, 256:384] = np.eye(128)
    return c


def _rope_tables(positions):
    invf = 1.0 / (10000.0 ** (np.arange(0, 32, 2, dtype=np.float64) / 32))
    f = positions[:, None].astype(np.float64) * invf[None, :]
    ct = np.tile(np.cos(f), (1, 16)).astype(np.float32)
    st = np.tile(np.sin(f), (1, 16)).astype(np.float32)
    return ct, st


def _gather_tokens(parts, bb):
    W = parts[0].shape[-1]
    out = np.empty((GT, 128, W), parts[0].dtype)
    for cc in range(4):
        out[cc::4] = np.asarray(parts[4 * bb + cc]).reshape(TB, 128, W)
    return out.reshape(SEQ, W)


_NC_CACHE = {}


def _get_nc(name, builder):
    if name not in _NC_CACHE:
        _NC_CACHE[name] = builder()
    return _NC_CACHE[name]


def kernel(L0_attn_hidden_states, L0_attn_W_qkv, L0_attn_b_qkv,
           L0_attn_W_dense, L0_attn_b_dense, L0_ffn_gate_w, L0_ffn_gate_b,
           L0_ffn_expert_w, L0_ffn_expert_b, L1_attn_w_qkv, L1_attn_b_qkv,
           L1_attn_w_proj, L1_attn_b_proj, L1_ffn_router_w, L1_ffn_wg,
           L1_ffn_wu, L1_ffn_wd, L1_ffn_sg, L1_ffn_su, L1_ffn_sd):
    consts = _consts_host()
    cores = list(range(8))
    f32 = lambda a: np.ascontiguousarray(np.asarray(a), dtype=np.float32)

    x = np.asarray(L0_attn_hidden_states)
    xs, rts = [], []
    for c in cores:
        bb, cc = divmod(c, 4)
        xs.append(np.ascontiguousarray(x[bb].reshape(GT, 128, D)[cc::4]))
        posn = ((np.arange(TB)[:, None] * 4 + cc) * 128 +
                np.arange(128)[None, :]).reshape(-1)
        rts.append(_rope_tables(posn))

    nc1 = _get_nc("L1", build_L1)
    in1 = [{
        "x_in": xs[c],
        "gate_wT": f32(np.asarray(L0_ffn_gate_w).T),
        "gate_b": f32(L0_ffn_gate_b).reshape(1, 8),
        "exp_w": f32(L0_ffn_expert_w),
        "exp_b": f32(L0_ffn_expert_b),
        "wqkv": f32(L0_attn_W_qkv),
        "bqkv": f32(L0_attn_b_qkv).reshape(1, -1),
        "ct": rts[c][0], "st": rts[c][1],
        "consts": consts,
    } for c in cores]
    r1 = run_bass_kernel_spmd(nc1, in1, core_ids=cores)
    x1 = [np.asarray(r1.results[c]["x1"]) for c in cores]
    qkv0 = [np.asarray(r1.results[c]["qkv"]) for c in cores]
    norms0 = [np.asarray(r1.results[c]["norms"]).reshape(32) for c in cores]

    def att_inputs(qkvs, normss, split):
        gath = [_gather_tokens(qkvs, bb) for bb in range(B)]
        ins = []
        for c in cores:
            qTs, kTs, vvs, nr = [], [], [], []
            for i in range(4):
                uu = 4 * c + i
                bb, h = divmod(uu, NH)
                g = gath[bb]
                if split:
                    qg = g[:, h * HD:(h + 1) * HD]
                    kg = g[:, D + h * HD: D + (h + 1) * HD]
                    vg = g[:, 2 * D + h * HD: 2 * D + (h + 1) * HD]
                else:
                    base = h * 384
                    qg = g[:, base: base + HD]
                    kg = g[:, base + HD: base + 2 * HD]
                    vg = g[:, base + 2 * HD: base + 3 * HD]
                qTs.append(np.ascontiguousarray(qg.T))
                kTs.append(np.ascontiguousarray(kg.T))
                vvs.append(np.ascontiguousarray(vg))
                for cc in range(4):
                    nr.append([normss[4 * bb + cc][h],
                               normss[4 * bb + cc][16 + h]])
            ins.append({"qT": np.stack(qTs), "kT": np.stack(kTs),
                        "vv": np.stack(vvs),
                        "nrm": np.asarray(nr, np.float32).reshape(1, 32),
                        "consts": consts})
        return ins

    nc2 = _get_nc("L2", lambda: build_att(True))
    r2 = run_bass_kernel_spmd(nc2, att_inputs(qkv0, norms0, False),
                              core_ids=cores)
    oT0u = [np.asarray(r2.results[c]["oT"]) for c in cores]

    def own_oT(c, oTu):
        bb, cc = divmod(c, 4)
        cols = (((np.arange(TB)[:, None] * 4 + cc) * 128 +
                 np.arange(128)[None, :]).reshape(-1))
        out = np.empty((NH, 128, TOK), BF)
        for h in range(NH):
            uu = bb * NH + h
            out[h] = oTu[uu // 4][uu % 4][:, cols]
        return out

    nc3 = _get_nc("L3", build_L3)
    in3 = [{
        "x_in": x1[c],
        "oT": own_oT(c, oT0u),
        "wdense": f32(L0_attn_W_dense),
        "bdense": f32(L0_attn_b_dense).reshape(1, -1),
        "gate_wT": f32(np.asarray(L0_ffn_gate_w).T),
        "gate_b": f32(L0_ffn_gate_b).reshape(1, 8),
        "exp_w": f32(L0_ffn_expert_w),
        "exp_b": f32(L0_ffn_expert_b),
        "rout_w": f32(L1_ffn_router_w),
        "f1wg": f32(L1_ffn_wg), "f1wu": f32(L1_ffn_wu), "f1wd": f32(L1_ffn_wd),
        "f1sg": f32(L1_ffn_sg), "f1su": f32(L1_ffn_su), "f1sd": f32(L1_ffn_sd),
        "wqkv": f32(L1_attn_w_qkv),
        "bqkv": f32(L1_attn_b_qkv).reshape(1, -1),
        "consts": consts,
    } for c in cores]
    r3 = run_bass_kernel_spmd(nc3, in3, core_ids=cores)
    x4 = [np.asarray(r3.results[c]["x4"]) for c in cores]
    qkv1 = [np.asarray(r3.results[c]["qkv"]) for c in cores]
    norms1 = [np.asarray(r3.results[c]["norms"]).reshape(32) for c in cores]

    nc4 = _get_nc("L4", lambda: build_att(False))
    r4 = run_bass_kernel_spmd(nc4, att_inputs(qkv1, norms1, True),
                              core_ids=cores)
    oT1u = [np.asarray(r4.results[c]["oT"]) for c in cores]

    nc5 = _get_nc("L5", build_L5)
    in5 = [{
        "x_in": x4[c],
        "oT": own_oT(c, oT1u),
        "wdense": f32(L1_attn_w_proj),
        "bdense": f32(L1_attn_b_proj).reshape(1, -1),
        "rout_w": f32(L1_ffn_router_w),
        "f1wg": f32(L1_ffn_wg), "f1wu": f32(L1_ffn_wu), "f1wd": f32(L1_ffn_wd),
        "f1sg": f32(L1_ffn_sg), "f1su": f32(L1_ffn_su), "f1sd": f32(L1_ffn_sd),
        "consts": consts,
    } for c in cores]
    r5 = run_bass_kernel_spmd(nc5, in5, core_ids=cores)

    out = np.empty((B, SEQ, D), np.float32)
    for c in cores:
        bb, cc = divmod(c, 4)
        out[bb].reshape(GT, 128, D)[cc::4] = np.asarray(r5.results[c]["x_out"])
    return out


# version 02
# speedup vs baseline: 1.1513x; 1.1513x over previous
import math
from contextlib import ExitStack

import numpy as np
import ml_dtypes

import concourse.bass as bass
import concourse.mybir as mybir
from concourse.bass_utils import run_bass_kernel_spmd

F32 = mybir.dt.float32
BF16 = mybir.dt.bfloat16
AF = mybir.ActivationFunctionType
ALU = mybir.AluOpType
AX = mybir.AxisListType

B, SEQ, D = 2, 8192, 2048
TOK, TB, NDC = 2048, 16, 16
NH, HD = 16, 128
NU = 4
E0 = 8
H1, NHC1, HS = 1408, 11, 2816
SCALE = 1.0 / math.sqrt(HD)
BF = ml_dtypes.bfloat16
RG = [[0, 1, 2, 3], [4, 5, 6, 7]]

XNT_COLS = 32768
WORK_COLS = 24576
ARENA_COLS = XNT_COLS + WORK_COLS

ENGS = ("pe", "dv", "ac", "gp", "sp")


class S:
    __slots__ = ("name", "count")

    def __init__(self, name):
        self.name, self.count = name, 0


class Prog:
    def __init__(self, nc):
        self.nc = nc
        self.stream = {e: [] for e in ENGS}
        self.sems = {}
        self.hwm = {e: {} for e in ENGS}

    def sem(self, name):
        if name not in self.sems:
            self.sems[name] = S(name)
        return self.sems[name]

    def op(self, eng, method, *args, waits=(), inc=None, inc_n=1, **kwargs):
        filt = []
        for s, v in waits:
            if v is None or v <= 0:
                continue
            if v > self.hwm[eng].get(s.name, 0):
                filt.append((s, v))
                self.hwm[eng][s.name] = v
        self.stream[eng].append((method, args, kwargs, filt, inc, inc_n))
        if inc is not None:
            inc.count += inc_n
            return inc.count
        return None

    def dma(self, eng, out, in_, sem, waits=(), **kw):
        return self.op(eng, "dma_start", waits=waits, inc=sem, inc_n=16,
                       out=out, in_=in_, **kw)

    def replay(self, engname, engine, handles):
        for method, args, kwargs, waits, inc, inc_n in self.stream[engname]:
            for s, v in waits:
                engine.wait_ge(handles[s.name], v)
            if callable(method):
                if getattr(method, "wants_handles", False):
                    inst = method(engine, handles)
                else:
                    inst = method(engine)
            else:
                inst = getattr(engine, method)(*args, **kwargs)
            if inc is not None:
                assert inst is not None
                inst.then_inc(handles[inc.name], inc_n)

    def finalize(self, stack, final_waits):
        nc = self.nc
        handles = {name: stack.enter_context(nc.semaphore(name))
                   for name in self.sems}
        block = stack.enter_context(nc.Block())
        p, fw = self, list(final_waits)

        @block.tensor
        def _(t):
            p.replay("pe", t, handles)

        @block.vector
        def _(v):
            p.replay("dv", v, handles)

        @block.scalar
        def _(s):
            p.replay("ac", s, handles)

        @block.gpsimd
        def _(g):
            p.replay("gp", g, handles)

        @block.sync
        def _(sy):
            p.replay("sp", sy, handles)
            for s, v in fw:
                sy.wait_ge(handles[s.name], v)


class Bufs:

    def __init__(self, nc, stack):
        self.nc, self.stack, self.t = nc, stack, {}
        self._ar = None

    def sb(self, name, shape, dtype):
        if name not in self.t:
            self.t[name] = self.stack.enter_context(
                self.nc.sbuf_tensor(name, shape, dtype))
        return self.t[name]

    def ps(self, name, shape, dtype):
        if name not in self.t:
            self.t[name] = self.stack.enter_context(
                self.nc.psum_tensor(name, shape, dtype))
        return self.t[name]

    def arena(self, c0, ncols, nrow=128, f32=False):
        if self._ar is None:
            self._ar = self.stack.enter_context(
                self.nc.sbuf_tensor("arena", [128, ARENA_COLS], BF16))
        v = self._ar[0:nrow, c0:c0 + ncols]
        if f32:
            v = v.bitcast(F32)
        return v

    def work(self, c0, ncols, nrow=128, f32=False):
        return self.arena(XNT_COLS + c0, ncols, nrow=nrow, f32=f32)



def register_consts(p, b):
    cz = b.sb("constz", [128, 2], F32)
    s_gp = p.sem("s_gp")
    p.op("gp", "memset", cz[:, 0:1], 0.0, inc=s_gp)
    p.op("gp", "memset", cz[:, 1:2], 1e-5, inc=s_gp)
    b.nc.const_aps.register(F32, 0.0, cz[:, 0:1])
    b.nc.const_aps.register(F32, 1e-5, cz[:, 1:2])


def alloc_common(p, b):
    for i in range(4):
        b.ps(f"ps{i}", [128, 512], F32)
    b.ps("ps4", [128, 512], F32)
    b.ps("ps5", [128, 512], F32)
    b.ps("psE", [128, 512], F32)
    b.ps("psF", [128, 512], F32)
    b.t["ptrA"] = b.t["psE"][:].bitcast(BF16)
    b.t["ptrB"] = b.t["psF"][:].bitcast(BF16)


def emit_consts(p, b, consts):
    ones1 = b.sb("ones1", [1, 512], BF16)
    ones128 = b.sb("ones128", [128, 1], BF16)
    idb = b.sb("idb", [128, 128], BF16)
    idf = b.sb("idf", [128, 128], F32)
    trib = b.sb("trib", [128, 128], BF16)
    s_gp, s_ld = p.sem("s_gp"), p.sem("s_ldc")
    register_consts(p, b)
    p.op("gp", "memset", ones1[:], 1.0, inc=s_gp)
    p.op("gp", "memset", ones128[:], 1.0, inc=s_gp)
    p.dma("gp", idb[:], consts[:, 0:128], s_ld)
    p.dma("gp", trib[:], consts[:, 128:256], s_ld)
    p.dma("sp", idf[:], consts[:, 256:384], s_ld)
    return [(s_ld, s_ld.count), (s_gp, s_gp.count)]


def emit_load_x(p, b, x_ap):
    x_sb = b.sb("x_sb", [128, TB * D], BF16)
    s = p.sem("s_ldx")
    for tb in range(TB):
        p.dma("gp", x_sb[:, tb * D:(tb + 1) * D], x_ap[tb], s)
    return [(s, s.count)]


def emit_store_x(p, b, out_ap, waits):
    s = p.sem("s_stx")
    x_sb = b.t["x_sb"]
    for tb in range(TB):
        p.dma("gp", out_ap[tb], x_sb[:, tb * D:(tb + 1) * D], s, waits=waits)
    return [(s, s.count)]


def emit_ln(p, b, x_ready, consts_ready):
    x_sb = b.t["x_sb"]
    xnT = b.arena(0, NDC * TOK)
    b.t["xnT"] = xnT
    xn_st = b.work(16384, 2 * D)
    bnst = b.sb("bnst", [128, TB * 24], F32)
    mv = b.sb("mv", [128, TB * 2], F32)
    stat = b.sb("stat", [128, 2 * TB], F32)
    ptrA, ptrB = b.t["ptrA"], b.t["ptrB"]
    s_dv, s_ac, s_pe = p.sem("s_dv"), p.sem("s_ac"), p.sem("s_pe")

    t_stats = None
    for tb in range(TB):
        for ch in range(4):
            t_stats = p.op("dv", "bn_stats",
                           bnst[:, tb * 24 + ch * 6: tb * 24 + (ch + 1) * 6],
                           x_sb[:, tb * D + ch * 512: tb * D + (ch + 1) * 512],
                           waits=x_ready,
                           inc=s_dv if (tb, ch) == (TB - 1, 3) else None)
    t_aggr = None
    for tb in range(TB):
        t_aggr = p.op("dv", "bn_aggr", mv[:, tb * 2: tb * 2 + 2],
                      bnst[:, tb * 24:(tb + 1) * 24],
                      waits=[(s_dv, t_stats)],
                      inc=s_dv if tb == TB - 1 else None)
    varv = mv[:].rearrange("p (t two) -> p t two", two=2)[:, :, 1:2]
    t_sqrt = p.op("ac", "activation", stat[:, 0:TB], varv, AF.Sqrt, bias=1e-5,
                  waits=[(s_dv, t_aggr)], inc=s_ac)
    t_rstd = p.op("dv", "reciprocal", stat[:, TB:2 * TB], stat[:, 0:TB],
                  waits=[(s_ac, t_sqrt)], inc=s_dv)

    tp_done, ev_done = {}, {}
    for tb in range(TB):
        par = tb % 2
        w = [(s_dv, t_rstd)] + x_ready
        if tb >= 2:
            w.append((s_pe, tp_done[(tb - 2, 1)]))
        t_app = p.op("dv", "tensor_scalar", xn_st[:, par * D:(par + 1) * D],
                     x_sb[:, tb * D:(tb + 1) * D],
                     mv[:, tb * 2: tb * 2 + 1], stat[:, TB + tb: TB + tb + 1],
                     ALU.subtract, ALU.mult, waits=w, inc=s_dv)
        for grp in range(2):
            bank = ptrA if grp == 0 else ptrB
            wts = [(s_dv, t_app)] + consts_ready
            if (tb - 1, grp) in ev_done:
                wts.append((s_dv, ev_done[(tb - 1, grp)]))
            tv = None
            for k in range(8):
                dc = grp * 8 + k
                tv = p.op("pe", "transpose", bank[:, k * 128:(k + 1) * 128],
                          xn_st[:, par * D + dc * 128: par * D + (dc + 1) * 128],
                          b.t["idb"][:], waits=wts,
                          inc=s_pe if k == 7 else None)
            tp_done[(tb, grp)] = tv
            ev_done[(tb, grp)] = p.op(
                "dv", "tensor_copy",
                xnT[:].rearrange("p (dc t) -> p dc t", dc=NDC)[
                    :, grp * 8:(grp + 1) * 8, tb * 128:(tb + 1) * 128],
                bank[:].rearrange("p (k t) -> p k t", k=8),
                waits=[(s_pe, tv)], inc=s_dv)
    return [(s_dv, p.sems["s_dv"].count)]


def xnT_mv(b, dc, t0, n):
    return b.t["xnT"][:, dc * TOK + t0: dc * TOK + t0 + n]


def xnT_st(b, dc, tb):
    return b.t["xnT"][:, dc * TOK + tb * 128: dc * TOK + (tb + 1) * 128]


def emit_gating(p, b, kind, tag, gate_wT_ap, gate_b_ap, xnT_ready,
                consts_ready):
    gwr = b.work(0, D, nrow=8)
    lgT = b.work(2048, 4096, nrow=8, f32=True)
    gwT = b.sb("gwT", [128, NDC * 8], BF16)
    gb = b.sb("gb", [1, 8], BF16)
    lg = b.sb("lg", [128, TB * 8], F32)
    top8 = b.sb("top8", [128, TB * 8], F32)
    wg = b.sb("wgate" + tag, [128, TB * 8], F32)
    ps_g, ps_t = b.t["ps4"], b.t["ps5"]
    ptrA = b.t["ptrA"]
    s_ld = p.sem("s_ldg" + tag)
    s_pe, s_dv, s_ac = p.sem("s_pe"), p.sem("s_dv"), p.sem("s_ac")

    p.dma("gp", gwr, gate_wT_ap, s_ld, waits=consts_ready + xnT_ready)
    if gate_b_ap is not None:
        p.dma("gp", gb[:], gate_b_ap, s_ld)
    v_w = s_ld.count

    for grp in range(2):
        tv = None
        for k in range(8):
            dc = grp * 8 + k
            tv = p.op("pe", "transpose", ptrA[:, k * 128: k * 128 + 8],
                      gwr[:, dc * 128:(dc + 1) * 128], b.t["idb"][0:8, 0:8],
                      waits=[(s_ld, v_w)] + xnT_ready +
                      ([(s_dv, p.sems["s_dv"].count)] if grp else []),
                      inc=s_pe if k == 7 else None)
        p.op("dv", "tensor_copy",
             gwT[:].rearrange("p (dc e) -> p dc e", e=8)[:, grp * 8:(grp + 1) * 8, :],
             ptrA[:].rearrange("p (k t) -> p k t", k=8)[:, :, 0:8],
             waits=[(s_pe, tv)], inc=s_dv)
    t_gwT = p.sems["s_dv"].count

    prev_ev = None
    for tc in range(4):
        def mmfn(t, tc=tc):
            last = None
            if gate_b_ap is not None:
                last = t.matmul(ps_g[0:8, 0:512], gb[0:1, :],
                                b.t["ones1"][0:1, 0:512], start=True, stop=False)
            for dc in range(NDC):
                last = t.matmul(ps_g[0:8, 0:512], gwT[:, dc * 8:(dc + 1) * 8],
                                xnT_mv(b, dc, tc * 512, 512),
                                start=(dc == 0 and gate_b_ap is None),
                                stop=(dc == NDC - 1))
            return last
        w = xnT_ready + [(s_dv, t_gwT)]
        if prev_ev is not None:
            w.append((s_ac, prev_ev))
        tv = p.op("pe", mmfn, waits=w, inc=s_pe)
        prev_ev = p.op("ac", "copy", lgT[:, tc * 512:(tc + 1) * 512],
                       ps_g[0:8, 0:512], waits=[(s_pe, tv)], inc=s_ac)

    tv = None
    for tb in range(TB):
        tv = p.op("pe", "transpose", ps_t[:, tb * 8: tb * 8 + 8],
                  lgT[:, tb * 128:(tb + 1) * 128], b.t["idf"][0:8, 0:8],
                  waits=[(s_ac, prev_ev)], inc=s_pe if tb == TB - 1 else None)
    t_lg = p.op("dv", "tensor_copy", lg[:], ps_t[:, 0:TB * 8],
                waits=[(s_pe, tv)], inc=s_dv)

    if kind == "ffn0":
        t_m = None
        for tb in range(TB):
            t_m = p.op("dv", "max", top8[:, tb * 8:(tb + 1) * 8],
                       lg[:, tb * 8:(tb + 1) * 8], waits=[(s_dv, t_lg)],
                       inc=s_dv if tb == TB - 1 else None)
        for tb in range(TB):
            p.op("dv", "tensor_scalar", wg[:, tb * 8:(tb + 1) * 8],
                 lg[:, tb * 8:(tb + 1) * 8],
                 top8[:, tb * 8 + 1: tb * 8 + 2], 0.25, ALU.is_ge, ALU.mult,
                 waits=[(s_dv, t_m)], inc=s_dv if tb == TB - 1 else None)
    else:
        nm = b.sb("g1nm", [128, TB], F32)
        ex = b.sb("g1ex", [128, TB * 8], F32)
        lsum = b.sb("g1ls", [128, TB], F32)
        rl = b.sb("g1rl", [128, TB], F32)
        pv = b.sb("g1pv", [128, TB * 8], F32)
        den = b.sb("g1den", [128, TB], F32)
        t1 = None
        for tb in range(TB):
            t1 = p.op("dv", "tensor_reduce", nm[:, tb: tb + 1],
                      lg[:, tb * 8:(tb + 1) * 8], AX.X, ALU.max, negate=True,
                      waits=[(s_dv, t_lg)], inc=s_dv if tb == TB - 1 else None)
        t2 = None
        for tb in range(TB):
            t2 = p.op("ac", "activation", ex[:, tb * 8:(tb + 1) * 8],
                      lg[:, tb * 8:(tb + 1) * 8], AF.Exp,
                      bias=nm[:, tb: tb + 1], accum_out=lsum[:, tb: tb + 1],
                      waits=[(s_dv, t1)], inc=s_ac if tb == TB - 1 else None)
        t3 = p.op("dv", "reciprocal", rl[:], lsum[:], waits=[(s_ac, t2)],
                  inc=s_dv)
        t4 = None
        for tb in range(TB):
            t4 = p.op("dv", "tensor_scalar_mul", pv[:, tb * 8:(tb + 1) * 8],
                      ex[:, tb * 8:(tb + 1) * 8], rl[:, tb: tb + 1],
                      waits=[(s_dv, t3)], inc=s_dv if tb == TB - 1 else None)
        t5 = None
        for tb in range(TB):
            t5 = p.op("dv", "max", top8[:, tb * 8:(tb + 1) * 8],
                      pv[:, tb * 8:(tb + 1) * 8], waits=[(s_dv, t4)],
                      inc=s_dv if tb == TB - 1 else None)
        t6 = None
        for tb in range(TB):
            t6 = p.op("dv", "tensor_scalar", den[:, tb: tb + 1],
                      top8[:, tb * 8: tb * 8 + 1],
                      top8[:, tb * 8 + 1: tb * 8 + 2], 1e-20,
                      ALU.add, ALU.add, waits=[(s_dv, t5)],
                      inc=s_dv if tb == TB - 1 else None)
        t7 = p.op("dv", "reciprocal", den[:], den[:], waits=[(s_dv, t6)],
                  inc=s_dv)
        t7b = p.op("dv", "tensor_scalar_mul", den[:], den[:], 0.5,
                   waits=[(s_dv, t7)], inc=s_dv)
        t8 = None
        for tb in range(TB):
            t8 = p.op("dv", "tensor_scalar", wg[:, tb * 8:(tb + 1) * 8],
                      pv[:, tb * 8:(tb + 1) * 8],
                      top8[:, tb * 8 + 1: tb * 8 + 2], None, ALU.is_ge,
                      waits=[(s_dv, t7b)], inc=s_dv if tb == TB - 1 else None)
        for tb in range(TB):
            p.op("dv", "scalar_tensor_tensor", wg[:, tb * 8:(tb + 1) * 8],
                 pv[:, tb * 8:(tb + 1) * 8], den[:, tb: tb + 1],
                 wg[:, tb * 8:(tb + 1) * 8], ALU.mult, ALU.mult,
                 waits=[(s_dv, t8)], inc=s_dv if tb == TB - 1 else None)
    return wg, [(s_dv, p.sems["s_dv"].count)]


def emit_ffn0(p, b, tag, expert_w_ap, expert_b_ap, wg, gate_ready, xnT_ready):
    x_sb = b.t["x_sb"]
    wbuf = [b.work(0, NDC * 512), b.work(NDC * 512, NDC * 512)]
    be = b.work(16384, D, nrow=1)
    tmp = [b.sb(f"f0t{i}", [128, 512], BF16) for i in range(2)]
    ps = [b.t[f"ps{i}"] for i in range(4)]
    s_w = [p.sem(f"s_f0w{tag}0"), p.sem(f"s_f0w{tag}1")]
    s_b = p.sem(f"s_f0b{tag}")
    s_pe, s_dv, s_ac = p.sem("s_pe"), p.sem("s_dv"), p.sem("s_ac")

    mm_done, ev_done = {}, {}
    idx = 0
    for e in range(E0):
        bw = [(s_pe, mm_done[(e - 1, 3)])] if e else (gate_ready + xnT_ready)
        bd = p.dma("gp", be, expert_b_ap[e: e + 1, :], s_b, waits=bw)
        for oc in range(4):
            par = (e * 4 + oc) % 2
            prev = (e, oc - 2) if oc >= 2 else (e - 1, oc + 2)
            w = [(s_b, bd)]
            if prev in mm_done:
                w.append((s_pe, mm_done[prev]))
            wd = p.dma("gp", wbuf[par].rearrange("p (dc o) -> p dc o", o=512),
                       expert_w_ap[e][:, oc * 512:(oc + 1) * 512]
                       .rearrange("(dc p) o -> p dc o", p=128),
                       s_w[par], waits=w)
            for tb in range(TB):
                bank = ps[idx % 4]
                w = xnT_ready + [(s_w[par], wd)]
                if idx - 4 in ev_done:
                    w.append((s_dv, ev_done[idx - 4]))

                def mmfn(t, oc=oc, par=par, bank=bank, tb=tb):
                    t.matmul(bank[:], b.t["ones1"][0:1, 0:128],
                             be[0:1, oc * 512:(oc + 1) * 512],
                             start=True, stop=False)
                    last = None
                    for dc in range(NDC):
                        last = t.matmul(bank[:], xnT_st(b, dc, tb),
                                        wbuf[par][:, dc * 512:(dc + 1) * 512],
                                        start=False, stop=(dc == NDC - 1))
                    return last
                tv = p.op("pe", mmfn, waits=w, inc=s_pe)
                if tb == TB - 1:
                    mm_done[(e, oc)] = tv
                slot = idx % 2
                w2 = [(s_pe, tv)] + gate_ready
                if idx - 2 in ev_done:
                    w2.append((s_dv, ev_done[idx - 2]))
                ta = p.op("ac", "activation", tmp[slot][:], bank[:], AF.Relu,
                          scale=wg[:, tb * 8 + e: tb * 8 + e + 1],
                          waits=w2, inc=s_ac)
                xs = x_sb[:, tb * D + oc * 512: tb * D + (oc + 1) * 512]
                ev_done[idx] = p.op("dv", "tensor_add", xs, xs, tmp[slot][:],
                                    waits=[(s_ac, ta)], inc=s_dv)
                idx += 1
    return [(s_dv, p.sems["s_dv"].count)]


def emit_ffn1(p, b, tag, experts, xnT_ready):
    x_sb = b.t["x_sb"]
    gbuf = [b.work(i * 2048, 2048) for i in range(2)]
    ubuf = [b.work(4096 + i * 2048, 2048) for i in range(2)]
    dbuf = [b.work(8192 + i * 2048, 2048) for i in range(4)]
    hsb = b.work(16384, 4 * 2048)
    sil = [b.sb(f"f1s{i}", [128, 512], BF16) for i in range(2)]
    ps = [b.t[f"ps{i}"] for i in range(4)]
    s_g = [p.sem(f"s_f1g{tag}0"), p.sem(f"s_f1g{tag}1")]
    s_u = [p.sem(f"s_f1u{tag}0"), p.sem(f"s_f1u{tag}1")]
    s_d = [p.sem(f"s_f1d{tag}{i}") for i in range(4)]
    s_pe, s_dv, s_ac = p.sem("s_pe"), p.sem("s_dv"), p.sem("s_ac")

    groups = [list(range(0, 4)), list(range(4, 8)), list(range(8, NHC1))]
    prev_down_mm = None
    down_ev = None
    comb = {}
    gu_last = {}
    for ei, (wg_ap, wu_ap, wd_ap, gate) in enumerate(experts):
        for gi, grp in enumerate(groups):
            nhc = len(grp)
            comb.clear()
            gu_last.clear()
            for li, hc in enumerate(grp):
                par = li % 2
                w = list(xnT_ready)
                if down_ev is not None:
                    w.append((s_dv, down_ev))
                if prev_down_mm is not None:
                    w.append((s_pe, prev_down_mm))
                if (par, "last") in gu_last:
                    w.append((s_dv, gu_last[(par, "last")]))
                gd = p.dma("gp", gbuf[par].rearrange("p (dc h) -> p dc h", h=128),
                           wg_ap[:, hc * 128:(hc + 1) * 128]
                           .rearrange("(dc p) h -> p dc h", p=128),
                           s_g[par], waits=w)
                ud = p.dma("gp", ubuf[par].rearrange("p (dc h) -> p dc h", h=128),
                           wu_ap[:, hc * 128:(hc + 1) * 128]
                           .rearrange("(dc p) h -> p dc h", p=128),
                           s_u[par], waits=w)
                for tc in range(4):
                    bA = ps[(tc % 2) * 2]
                    bU = ps[(tc % 2) * 2 + 1]
                    w2 = xnT_ready + [(s_g[par], gd), (s_u[par], ud)]
                    guard = (li, tc - 2) if tc >= 2 else (li - 1, tc + 2)
                    if guard in comb:
                        w2.append((s_dv, comb[guard]))
                    if down_ev is not None:
                        w2.append((s_dv, down_ev))
                    if prev_down_mm is not None:
                        w2.append((s_pe, prev_down_mm))

                    def gmm(t, par=par, tc=tc, bA=bA):
                        last = None
                        for dc in range(NDC):
                            last = t.matmul(
                                bA[:], gbuf[par][:, dc * 128:(dc + 1) * 128],
                                xnT_mv(b, dc, tc * 512, 512),
                                start=(dc == 0), stop=(dc == NDC - 1))
                        return last

                    def umm(t, par=par, tc=tc, bU=bU):
                        last = None
                        for dc in range(NDC):
                            last = t.matmul(
                                bU[:], ubuf[par][:, dc * 128:(dc + 1) * 128],
                                xnT_mv(b, dc, tc * 512, 512),
                                start=(dc == 0), stop=(dc == NDC - 1))
                        return last
                    tg = p.op("pe", gmm, waits=w2, inc=s_pe)
                    tu = p.op("pe", umm, waits=w2, inc=s_pe)
                    slot = tc % 2
                    w3 = [(s_pe, tg)]
                    prevc = comb.get((li, tc - 2), comb.get((li - 1, tc + 2)))
                    if prevc is not None:
                        w3.append((s_dv, prevc))
                    tsil = p.op("ac", "activation", sil[slot][:], bA[:],
                                AF.Silu, waits=w3, inc=s_ac)
                    cc = p.op("dv", "tensor_mul",
                              hsb[:, li * TOK + tc * 512: li * TOK + (tc + 1) * 512],
                              sil[slot][:], bU[:],
                              waits=[(s_pe, tu), (s_ac, tsil)], inc=s_dv)
                    comb[(li, tc)] = cc
                gu_last[(par, "last")] = comb[(li, 3)]
            comb_all = p.sems["s_dv"].count
            dld = []
            for li, hc in enumerate(grp):
                w = [(s_pe, prev_down_mm)] if prev_down_mm else []
                dld.append(p.dma("gp", dbuf[li],
                                 wd_ap[hc * 128:(hc + 1) * 128, :],
                                 s_d[li], waits=w))
            last_mm = None
            prev_ev = down_ev
            for tb in range(TB):
                w = [(s_dv, comb_all)] + \
                    [(s_d[li], dld[li]) for li in range(nhc)]
                if prev_ev is not None:
                    w.append((s_dv, prev_ev))

                def dmm(t, tb=tb, nhc=nhc):
                    last = None
                    for li in range(nhc):
                        for dcol in range(4):
                            last = t.matmul(
                                ps[dcol][:],
                                hsb[:, li * TOK + tb * 128: li * TOK + (tb + 1) * 128],
                                dbuf[li][:, dcol * 512:(dcol + 1) * 512],
                                start=(li == 0), stop=(li == nhc - 1))
                    return last
                last_mm = p.op("pe", dmm, waits=w, inc=s_pe)
                for dcol in range(4):
                    xs = x_sb[:, tb * D + dcol * 512: tb * D + (dcol + 1) * 512]
                    if gate[0] == "col":
                        sc_ = gate[1][:, tb * 8 + gate[2]: tb * 8 + gate[2] + 1]
                    else:
                        sc_ = gate[1]
                    prev_ev = p.op("dv", "scalar_tensor_tensor", xs,
                                   ps[dcol][:], sc_, xs, ALU.mult, ALU.add,
                                   waits=[(s_pe, last_mm)], inc=s_dv)
            prev_down_mm = last_mm
            down_ev = prev_ev
    return [(s_dv, p.sems["s_dv"].count)]


def emit_xnt_ag(p, b, tag, xl_t, xg_t, xnT_ready):
    xl_ap = xl_t.ap()
    s_st = p.sem("s_xst" + tag)
    xnT = b.t["xnT"]
    last = None
    for dc in range(NDC):
        last = p.dma("sp", xl_ap[dc * 128:(dc + 1) * 128, :],
                     xnT[:, dc * TOK:(dc + 1) * TOK], s_st, waits=xnT_ready)
    s_cc = p.sem("s_cc" + tag)

    def fn(g):
        return g.collective_compute(
            "AllGather", ALU.bypass, replica_groups=RG,
            ins=[xl_t.ap().opt()], outs=[xg_t.ap().opt()])
    v = p.op("gp", fn, waits=[(s_st, last)], inc=s_cc)
    return [(s_cc, v)]


def emit_qkv_units(p, b, tag, rope, wq_ap, bq_ap, xg_ap, qkvT_ap,
                   ct_ap, st_ap, ag_ready, region_free):
    warr = b.work(0, 12 * NDC * 128)
    cb = [b.arena(0, NDC * 512), b.arena(NDC * 512, NDC * 512)]
    stg = [b.arena(16384 + i * 512, 512) for i in range(4)]
    sqt = [b.arena(18432 + i * 512, 512) for i in range(2)]
    bqt = b.arena(19456, 12 * 128, nrow=1)
    nacc = b.sb("nacc" + tag, [1, 12], F32)
    ntmp = b.sb("ntmp" + tag, [1, 1], F32)
    negm = b.sb("negm" + tag, [1, NU], F32)
    negmb = b.sb("negmb" + tag, [1, NU], BF16)
    negM = b.sb("negM" + tag, [128, NU], F32)
    ps = [b.t[f"ps{i}"] for i in range(4)]
    ps_n = [b.t["ps4"], b.t["ps5"]]
    s_w = p.sem("s_uw" + tag)
    s_cx = p.sem("s_ucx" + tag)
    s_st = p.sem("s_ust" + tag)
    s_pe, s_dv, s_ac = p.sem("s_pe"), p.sem("s_dv"), p.sem("s_ac")
    s_gp = p.sem("s_gp")

    wcnt = {}
    for fb in range(12):
        for dc in range(NDC):
            wcnt[fb] = p.dma(
                "gp", warr[:, (fb * NDC + dc) * 128:(fb * NDC + dc + 1) * 128],
                wq_ap[dc * 128:(dc + 1) * 128, fb * 128:(fb + 1) * 128],
                s_w, waits=region_free if (fb == 0 and dc == 0) else ())
    bload = p.dma("gp", bqt, bq_ap, s_w, waits=ag_ready)

    t_z = p.op("gp", "memset", nacc[:], 0.0, inc=s_gp)

    mm_marks = {}
    ev_marks = {}
    st_marks = {}
    sq_marks = {}
    tn_marks = {}
    tr_marks = {}
    nred_prev = None
    nrm_pend = None
    idx = 0
    nqk = 0

    def emit_norm(pfb, pnqk):
        nonlocal nred_prev
        nb = ps_n[pnqk % 2]
        wtn = [(s_ac, sq_marks[pnqk])]
        if pnqk - 2 in tr_marks:
            wtn.append((s_dv, tr_marks[pnqk - 2]))
        tn = p.op("pe", lambda t, nb=nb, pnqk=pnqk: t.matmul(
            nb[0:1, 0:512], b.t["ones128"][:], sqt[pnqk % 2][:],
            start=True, stop=True),
            waits=wtn, inc=s_pe)
        tn_marks[pnqk] = tn
        w = [(s_pe, tn), (s_gp, t_z)]
        if nred_prev is not None:
            w.append((s_dv, nred_prev))
        tr = p.op("dv", "tensor_reduce", ntmp[:, 0:1],
                  nb[0:1, 0:512], AX.X, ALU.max, waits=w, inc=s_dv)
        tr_marks[pnqk] = tr
        nred_prev = p.op("dv", "tensor_max", nacc[:, pfb:pfb + 1],
                         nacc[:, pfb:pfb + 1], ntmp[:, 0:1],
                         waits=[(s_dv, tr)], inc=s_dv)

    for ch in range(16):
        par = ch % 2
        r, cp = divmod(ch, 4)
        w = list(ag_ready) + region_free
        if ch >= 2:
            w.append((s_pe, mm_marks[(ch - 2, 11)]))
        ld = None
        for dc in range(NDC):
            ld = p.dma("sp", cb[par][:, dc * 512:(dc + 1) * 512],
                       xg_ap[r * D + dc * 128: r * D + (dc + 1) * 128,
                             cp * 512:(cp + 1) * 512],
                       s_cx, waits=w if dc == 0 else ())
        for fb in range(12):
            bank = ps[idx % 4]
            w2 = [(s_cx, ld), (s_w, max(wcnt[fb], bload))]
            if idx - 4 in ev_marks:
                w2.append((s_ac, ev_marks[idx - 4]))

            def mmfn(t, fb=fb, par=par, bank=bank):
                t.matmul(bank[:], bqt[0:1, fb * 128:(fb + 1) * 128],
                         b.t["ones1"][0:1, 0:512], start=True, stop=False)
                last = None
                for dc in range(NDC):
                    last = t.matmul(
                        bank[:],
                        warr[:, (fb * NDC + dc) * 128:(fb * NDC + dc + 1) * 128],
                        cb[par][:, dc * 512:(dc + 1) * 512],
                        start=False, stop=(dc == NDC - 1))
                return last
            tv = p.op("pe", mmfn, waits=w2, inc=s_pe)
            mm_marks[(ch, fb)] = tv
            if nrm_pend is not None:
                emit_norm(*nrm_pend)
                nrm_pend = None
            slot = idx % 4
            w3 = [(s_pe, tv)]
            if idx - 4 in st_marks:
                w3.append((s_st, st_marks[idx - 4]))
            ev = p.op("ac", "copy", stg[slot][:], bank[:], waits=w3, inc=s_ac)
            ev_marks[idx] = ev
            st_marks[idx] = p.dma(
                "sp", qkvT_ap[fb * 128:(fb + 1) * 128,
                              ch * 512:(ch + 1) * 512],
                stg[slot][:], s_st, waits=[(s_ac, ev)])
            if fb % 3 != 2:
                wsq = [(s_ac, ev)]
                if nqk - 2 in tn_marks:
                    wsq.append((s_pe, tn_marks[nqk - 2]))
                sq = p.op("ac", "activation", sqt[nqk % 2][:],
                          stg[slot][:], AF.Square, waits=wsq, inc=s_ac)
                sq_marks[nqk] = sq
                nrm_pend = (fb, nqk)
                nqk += 1
            idx += 1
    if nrm_pend is not None:
        emit_norm(*nrm_pend)
        nrm_pend = None

    st_all = p.sems["s_ust" + tag].count
    sq_all = p.sems["s_ac"].count
    nrm_all = p.sems["s_dv"].count

    naccv = nacc[:].rearrange("p (u t) -> p u t", t=3)
    t_pr = p.op("dv", "tensor_mul",
                negm[:].rearrange("p (u o) -> p u o", o=1),
                naccv[:, :, 0:1], naccv[:, :, 1:2],
                waits=[(s_dv, nrm_all)], inc=s_dv)
    t_sq = p.op("ac", "activation", negm[:], negm[:], AF.Sqrt, bias=0.0,
                waits=[(s_dv, t_pr)], inc=s_ac)
    t_sc = p.op("dv", "tensor_scalar", negm[:], negm[:],
                -1.05 * SCALE, -1.0, ALU.mult, ALU.add,
                waits=[(s_ac, t_sq)], inc=s_dv)
    t_cb = p.op("dv", "tensor_copy", negmb[:], negm[:], waits=[(s_dv, t_sc)],
                inc=s_dv)
    t_bc = p.op("pe", lambda t: t.matmul(
        ps_n[0][:, 0:NU], b.t["ones1"][0:1, 0:128], negmb[0:1, :],
        start=True, stop=True), waits=[(s_dv, t_cb)], inc=s_pe)
    t_M = p.op("dv", "tensor_copy", negM[:], ps_n[0][:, 0:NU],
               waits=[(s_pe, t_bc)], inc=s_dv)
    negM_ready = [(s_dv, t_M)]

    if not rope:
        return [(s_st, st_all)], negM_ready, negM

    ctt = b.arena(0, 8192, nrow=16, f32=True)
    stt = b.arena(8192, 8192, nrow=16, f32=True)
    q0 = b.arena(16384, 4096, nrow=16)
    q1 = b.arena(20480, 4096, nrow=16)
    rA = b.arena(24576, 4096, nrow=16)
    rB = b.arena(28672, 4096, nrow=16)
    s_rl = p.sem("s_url" + tag)
    s_rs = p.sem("s_urs" + tag)
    afree = [(s_pe, mm_marks[(15, 11)]), (s_st, st_all), (s_ac, sq_all)]
    prev_ops = None
    for half in range(2):
        h0 = half * 4096
        wtb = list(afree)
        if prev_ops is not None:
            wtb.append((s_dv, prev_ops))
        tld = p.dma("sp", ctt, ct_ap[:, h0:h0 + 4096], s_rl, waits=wtb)
        tld = p.dma("sp", stt, st_ap[:, h0:h0 + 4096], s_rl)
        for u in range(NU):
            for qk in range(2):
                fb = 3 * u + qk
                row = fb * 128
                w = [(s_rl, tld)]
                if prev_ops is not None:
                    w.append((s_dv, prev_ops))
                l0 = p.dma("sp", q0[:], qkvT_ap[row:row + 16, h0:h0 + 4096],
                           s_rl, waits=w)
                l1 = p.dma("sp", q1[:], qkvT_ap[row + 16:row + 32, h0:h0 + 4096],
                           s_rl)
                lw = [(s_rl, l1)]
                m1 = p.op("dv", "tensor_mul", rA[:], q0[:], ctt, waits=lw,
                          inc=s_dv)
                m2 = p.op("dv", "tensor_mul", rB[:], q1[:], stt, waits=lw,
                          inc=s_dv)
                m3 = p.op("dv", "tensor_sub", rA[:], rA[:], rB[:],
                          waits=[(s_dv, m2)], inc=s_dv)
                sA = p.dma("sp", qkvT_ap[row:row + 16, h0:h0 + 4096], rA[:],
                           s_rs, waits=[(s_dv, m3)])
                m4 = p.op("dv", "tensor_mul", rB[:], q1[:], ctt,
                          waits=[(s_dv, m3)], inc=s_dv)
                m5 = p.op("dv", "tensor_mul", rA[:], q0[:], stt,
                          waits=[(s_rs, sA), (s_dv, m4)], inc=s_dv)
                m6 = p.op("dv", "tensor_add", rB[:], rB[:], rA[:],
                          waits=[(s_dv, m5)], inc=s_dv)
                sB = p.dma("sp", qkvT_ap[row + 16:row + 32, h0:h0 + 4096],
                           rB[:], s_rs, waits=[(s_dv, m6)])
                prev_ops = m6
    rs_all = p.sems["s_urs" + tag].count
    return [(s_st, st_all), (s_rs, rs_all)], negM_ready, negM


def emit_att(p, b, tag, causal, qkvT_ap, ot_ap, negM, qkv_ready, negM_ready,
             consts_ready):
    qsb = [b.arena(0, SEQ), b.arena(SEQ, SEQ)]
    ksb = [b.arena(2 * SEQ, SEQ), b.arena(3 * SEQ, SEQ)]
    vts = [b.work(0, SEQ), b.work(SEQ, SEQ)]
    vsb = b.work(2 * SEQ, SEQ)
    psl = [b.sb(f"apl{i}", [128, 512], BF16) for i in range(2)]
    ost = [b.sb(f"ao{i}", [128, 512], BF16) for i in range(2)]
    lsb = b.sb("al", [1, 512], F32)
    rlb = b.sb("arl", [1, 512], BF16)
    rbc = [b.sb(f"arb{i}", [128, 512], BF16) for i in range(2)]
    ps = [b.t[f"ps{i}"] for i in range(4)]
    psb = b.t["ps4"]
    pvb = [b.t["ps2"], b.t["psE"]]
    llb = [b.t["ps3"], b.t["psF"]]
    trib = b.t["trib"]
    s_ld = [p.sem(f"s_ald{tag}0"), p.sem(f"s_ald{tag}1")]
    s_st = [p.sem(f"s_ast{tag}0"), p.sem(f"s_ast{tag}1")]
    s_pe, s_dv, s_ac = p.sem("s_pe"), p.sem("s_dv"), p.sem("s_ac")

    unit_pe_done = {}
    vcopy_done = {}
    last_eo_dv = None
    for u in range(NU):
        par = u % 2
        w = list(qkv_ready) + consts_ready + negM_ready
        if u >= 2:
            w.append((s_pe, unit_pe_done[u - 2]))
        p.dma("sp", qsb[par][:], qkvT_ap[3 * u * 128:(3 * u + 1) * 128, :],
              s_ld[par], waits=w)
        p.dma("sp", ksb[par][:],
              qkvT_ap[(3 * u + 1) * 128:(3 * u + 2) * 128, :], s_ld[par])
        l3 = p.dma("sp", vts[par][:],
                   qkvT_ap[(3 * u + 2) * 128:(3 * u + 3) * 128, :],
                   s_ld[par])
        ldw = [(s_ld[par], l3)]
        vc_w = list(ldw) + consts_ready
        if u >= 1:
            vc_w.append((s_pe, unit_pe_done[u - 1]))
        if last_eo_dv is not None:
            vc_w.append((s_dv, last_eo_dv))
        for g in range(8):
            bank = b.t["ptrA"] if g % 2 == 0 else b.t["ptrB"]
            wts = list(vc_w)
            if (u, g - 2) in vcopy_done:
                wts.append((s_dv, vcopy_done[(u, g - 2)]))
            tv = None
            for i in range(8):
                tile = g * 8 + i
                tv = p.op("pe", "transpose", bank[:, i * 128:(i + 1) * 128],
                          vts[par][:, tile * 128:(tile + 1) * 128],
                          b.t["idb"][:], waits=wts,
                          inc=s_pe if i == 7 else None)
            vcopy_done[(u, g)] = p.op(
                "dv", "tensor_copy", vsb[:, g * 1024:(g + 1) * 1024],
                bank[:, 0:1024], waits=[(s_pe, tv)], inc=s_dv)
        v_ready = [(s_dv, vcopy_done[(u, 7)])]

        exp_done = {}
        msk_done = {}
        eo_marks = {}
        el_marks = {}
        ebc_prev = None
        for qc in range(16):
            nkt = 4 * qc + 4 if causal else 64
            qpar = qc % 2
            pvB, llB = pvb[qpar], llb[qpar]

            def emit_lpv(pj, poff, first, last, qc=qc, llB=llB, pvB=pvB):
                wp = [(s_dv, msk_done[(qc, pj)])] if (qc, pj) in msk_done \
                    else [(s_ac, exp_done[(qc, pj)])]
                wpl = list(wp)
                if first and (qc - 2) in el_marks:
                    wpl.append((s_dv, el_marks[qc - 2]))
                p.op("pe", lambda t, pj=pj, poff=poff, first=first,
                     last=last, llB=llB:
                     t.matmul(llB[0:1, poff:512], b.t["ones128"][:],
                              psl[pj % 2][:, poff:512],
                              start=first, stop=last),
                     waits=wpl, inc=s_pe)
                wpv = list(wp)
                if first and (qc - 2) in eo_marks:
                    wpv.append((s_dv, eo_marks[qc - 2]))
                return p.op(
                    "pe", lambda t, pj=pj, poff=poff, first=first,
                    last=last, pvB=pvB:
                    t.matmul(pvB[:, poff:512],
                             vsb[:, pj * 128:(pj + 1) * 128],
                             psl[pj % 2][:, poff:512],
                             start=first, stop=last),
                    waits=wpv, inc=s_pe)

            pend = None
            last_unit_mm = None
            for j in range(nkt):
                off = (max(0, j - 4 * qc) if causal else 0) * 128
                sl = j % 2
                sbank = ps[sl]
                w2 = ldw + v_ready
                if (qc, j - 2) in exp_done:
                    w2.append((s_ac, exp_done[(qc, j - 2)]))

                def smm(t, sbank=sbank, j=j, off=off, par=par, qc=qc):
                    return t.matmul(
                        sbank[:, off:512],
                        ksb[par][:, j * 128:(j + 1) * 128],
                        qsb[par][:, qc * 512 + off:(qc + 1) * 512],
                        start=True, stop=True)
                ts = p.op("pe", smm, waits=w2, inc=s_pe)
                te = p.op("ac", "activation", psl[sl][:, off:512],
                          sbank[:, off:512], AF.Exp, scale=SCALE,
                          bias=negM[:, u: u + 1],
                          waits=[(s_pe, ts)] + negM_ready, inc=s_ac)
                exp_done[(qc, j)] = te
                if causal and 0 <= j - 4 * qc < 4:
                    msk_done[(qc, j)] = p.op(
                        "dv", "tensor_mul", psl[sl][:, off: off + 128],
                        psl[sl][:, off: off + 128], trib[:],
                        waits=[(s_ac, te)] + consts_ready, inc=s_dv)
                if pend is not None:
                    last_unit_mm = emit_lpv(pend[0], pend[1],
                                            pend[0] == 0, False)
                pend = (j, off)
            last_unit_mm = emit_lpv(pend[0], pend[1], pend[0] == 0, True)

            el = p.op("dv", "tensor_copy", lsb[:], llB[0:1, 0:512],
                      waits=[(s_pe, last_unit_mm)], inc=s_dv)
            el_marks[qc] = el
            er = p.op("dv", "reciprocal", lsb[:], lsb[:],
                      waits=[(s_dv, el)], inc=s_dv)
            eb = p.op("dv", "tensor_copy", rlb[:], lsb[:],
                      waits=[(s_dv, er)], inc=s_dv)
            wb6 = [(s_dv, eb)]
            if ebc_prev is not None:
                wb6.append((s_dv, ebc_prev))
            tb6 = p.op("pe", lambda t: t.matmul(
                psb[:, 0:512], b.t["ones1"][0:1, 0:128], rlb[0:1, :],
                start=True, stop=True), waits=wb6, inc=s_pe)
            rslot = qc % 2
            ebc = p.op("dv", "tensor_copy", rbc[rslot][:], psb[:, 0:512],
                       waits=[(s_pe, tb6)], inc=s_dv)
            ebc_prev = ebc
            oslot = qc % 2
            w3 = [(s_dv, ebc)]
            if p.sems[f"s_ast{tag}{oslot}"].count > 0:
                w3.append((s_st[oslot], p.sems[f"s_ast{tag}{oslot}"].count))
            eo = p.op("dv", "scalar_tensor_tensor", ost[oslot][:],
                      pvB[:, 0:512], 1.0, rbc[rslot][:],
                      ALU.mult, ALU.mult, waits=w3, inc=s_dv)
            eo_marks[qc] = eo
            p.dma("sp", ot_ap[u * 128:(u + 1) * 128, qc * 512:(qc + 1) * 512],
                  ost[oslot][:], s_st[oslot], waits=[(s_dv, eo)])
        unit_pe_done[u] = p.sems["s_pe"].count
        last_eo_dv = p.sems["s_dv"].count
    return [(s_st[0], p.sems[f"s_ast{tag}0"].count),
            (s_st[1], p.sems[f"s_ast{tag}1"].count)]


def emit_ot_ag(p, b, tag, ot_t, og_t, att_done):
    s_cc = p.sem("s_occ" + tag)

    def fn(g):
        return g.collective_compute(
            "AllGather", ALU.bypass, replica_groups=RG,
            ins=[ot_t.ap().opt()], outs=[og_t.ap().opt()])
    v = p.op("gp", fn, waits=att_done, inc=s_cc)
    return [(s_cc, v)]


def emit_proj(p, b, tag, og_ap, wdense_ap, bdense_ap, x_ready, consts_ready,
              og_ready):
    x_sb = b.t["x_sb"]
    osb = b.work(0, 4 * TOK)
    dbf = [b.work(4 * TOK + i * D, D) for i in range(4)]
    bq = b.work(16384, D, nrow=1)
    ps = [b.t[f"ps{i}"] for i in range(4)]
    s_o = p.sem(f"s_pjo{tag}")
    s_wd = [p.sem(f"s_pjw{tag}{i}") for i in range(4)]
    s_b = p.sem(f"s_pjb{tag}")
    s_pe, s_dv = p.sem("s_pe"), p.sem("s_dv")

    bd = p.dma("gp", bq, bdense_ap, s_b,
               waits=consts_ready + x_ready + og_ready)
    prev_mm = None
    prev_ev = None
    for hg in range(4):
        w = ([(s_pe, prev_mm)] if prev_mm else []) + og_ready

        def ogload(engine, handles, hg=hg):
            for arm in engine.switch_core_id(n=128):
                cc = arm.logical % 4
                for i in range(4):
                    engine.dma_start(
                        out=osb[:, i * TOK:(i + 1) * TOK],
                        in_=og_ap[hg * 512 + i * 128: hg * 512 + (i + 1) * 128,
                                  cc * TOK:(cc + 1) * TOK]
                    ).then_inc(handles["s_pjo" + tag], 16)
            return None
        ogload.wants_handles = True
        p.op("sp", ogload, waits=w)
        s_o.count += 64
        olast = s_o.count
        wlast = []
        for i in range(4):
            h = hg * 4 + i
            wlast.append(p.dma("gp", dbf[i], wdense_ap[h * 128:(h + 1) * 128, :],
                               s_wd[i],
                               waits=([(s_pe, prev_mm)] if prev_mm
                                      else og_ready)))
        ow = [(s_o, olast)] + [(s_wd[i], wlast[i]) for i in range(4)]
        for tb in range(TB):
            w2 = ow + [(s_b, bd)] + x_ready
            if prev_ev is not None:
                w2.append((s_dv, prev_ev))

            def mmfn(t, hg=hg, tb=tb):
                last = None
                for dcol in range(4):
                    if hg == 0:
                        t.matmul(ps[dcol][:], b.t["ones1"][0:1, 0:128],
                                 bq[0:1, dcol * 512:(dcol + 1) * 512],
                                 start=True, stop=False)
                for i in range(4):
                    for dcol in range(4):
                        last = t.matmul(
                            ps[dcol][:],
                            osb[:, i * TOK + tb * 128: i * TOK + (tb + 1) * 128],
                            dbf[i][:, dcol * 512:(dcol + 1) * 512],
                            start=(hg != 0 and i == 0), stop=(i == 3))
                return last
            prev_mm = p.op("pe", mmfn, waits=w2, inc=s_pe)
            for dcol in range(4):
                xs = x_sb[:, tb * D + dcol * 512: tb * D + (dcol + 1) * 512]
                prev_ev = p.op("dv", "tensor_add", xs, xs, ps[dcol][:],
                               waits=[(s_pe, prev_mm)], inc=s_dv)
    return [(s_dv, p.sems["s_dv"].count)]



def build_fused(upto="all"):
    nc = bass.Bass("TRN2", target_bir_lowering=False, debug=False)
    x_in = nc.dram_tensor("x_in", [TB, 128, D], F32, kind="ExternalInput").ap()
    gate_wT = nc.dram_tensor("gate_wT", [8, D], F32, kind="ExternalInput").ap()
    gate_b = nc.dram_tensor("gate_b", [1, 8], F32, kind="ExternalInput").ap()
    exp_w = nc.dram_tensor("exp_w", [E0, D, D], F32, kind="ExternalInput").ap()
    exp_b = nc.dram_tensor("exp_b", [E0, D], F32, kind="ExternalInput").ap()
    wqkv0 = nc.dram_tensor("wqkv0", [D, 1536], F32, kind="ExternalInput").ap()
    bqkv0 = nc.dram_tensor("bqkv0", [1, 1536], F32, kind="ExternalInput").ap()
    wdense0 = nc.dram_tensor("wdense0", [D, D], F32, kind="ExternalInput").ap()
    bdense0 = nc.dram_tensor("bdense0", [1, D], F32, kind="ExternalInput").ap()
    ct_t = nc.dram_tensor("ct", [16, SEQ], F32, kind="ExternalInput").ap()
    st_t = nc.dram_tensor("st", [16, SEQ], F32, kind="ExternalInput").ap()
    rout_w = nc.dram_tensor("rout_w", [8, D], F32, kind="ExternalInput").ap()
    f1wg = nc.dram_tensor("f1wg", [E0, D, H1], F32, kind="ExternalInput").ap()
    f1wu = nc.dram_tensor("f1wu", [E0, D, H1], F32, kind="ExternalInput").ap()
    f1wd = nc.dram_tensor("f1wd", [E0, H1, D], F32, kind="ExternalInput").ap()
    f1sg = nc.dram_tensor("f1sg", [D, HS], F32, kind="ExternalInput").ap()
    f1su = nc.dram_tensor("f1su", [D, HS], F32, kind="ExternalInput").ap()
    f1sd = nc.dram_tensor("f1sd", [HS, D], F32, kind="ExternalInput").ap()
    wqkv1 = nc.dram_tensor("wqkv1", [D, 1536], F32, kind="ExternalInput").ap()
    bqkv1 = nc.dram_tensor("bqkv1", [1, 1536], F32, kind="ExternalInput").ap()
    wproj1 = nc.dram_tensor("wproj1", [D, D], F32, kind="ExternalInput").ap()
    bproj1 = nc.dram_tensor("bproj1", [1, D], F32, kind="ExternalInput").ap()
    consts = nc.dram_tensor("consts", [128, 384], F32, kind="ExternalInput").ap()
    x_out = nc.dram_tensor("x_out", [TB, 128, D], F32, kind="ExternalOutput").ap()

    xl = [nc.dram_tensor(f"xl{i}", [D, TOK], BF16) for i in range(2)]
    xg = [nc.dram_tensor(f"xg{i}", [4 * D, TOK], BF16) for i in range(2)]
    qkvT = [nc.dram_tensor(f"qkvT{i}", [1536, SEQ], BF16) for i in range(2)]
    ot = [nc.dram_tensor(f"ot{i}", [512, SEQ], BF16) for i in range(2)]
    og = [nc.dram_tensor(f"og{i}", [2048, SEQ], BF16) for i in range(2)]

    with ExitStack() as stack:
        p, b = Prog(nc), Bufs(nc, stack)
        alloc_common(p, b)
        cr = emit_consts(p, b, consts)
        xr = emit_load_x(p, b, x_in)
        fin = []

        ln1 = emit_ln(p, b, xr, cr)
        wg0, gr0 = emit_gating(p, b, "ffn0", "a", gate_wT, gate_b, ln1, cr)
        f0 = emit_ffn0(p, b, "a", exp_w, exp_b, wg0, gr0, ln1)
        if upto == "ffn0":
            stx = emit_store_x(p, b, x_out, f0)
            p.finalize(stack, stx)
            return nc
        ln2 = emit_ln(p, b, f0, cr)
        ag0 = emit_xnt_ag(p, b, "0", xl[0], xg[0], ln2)
        qk0, nM0, negM0 = emit_qkv_units(
            p, b, "0", True, wqkv0, bqkv0, xg[0].ap(), qkvT[0].ap(),
            ct_t, st_t, ag0, f0)
        if upto == "qkv0":
            stx = emit_store_x(p, b, x_out, f0)
            p.finalize(stack, stx + qk0 + nM0)
            return nc
        at0 = emit_att(p, b, "0", True, qkvT[0].ap(), ot[0].ap(), negM0,
                       qk0, nM0, cr)
        oag0 = emit_ot_ag(p, b, "0", ot[0], og[0], at0)
        pj0 = emit_proj(p, b, "0", og[0].ap(), wdense0, bdense0, ln2, cr, oag0)
        if upto == "att0":
            stx = emit_store_x(p, b, x_out, pj0)
            p.finalize(stack, stx)
            return nc
        ln3 = emit_ln(p, b, pj0, cr)
        wg0b, gr0b = emit_gating(p, b, "ffn0", "a", gate_wT, gate_b, ln3, cr)
        f0b = emit_ffn0(p, b, "a", exp_w, exp_b, wg0b, gr0b, ln3)
        ln4 = emit_ln(p, b, f0b, cr)
        wg1, gr1 = emit_gating(p, b, "ffn1", "b", rout_w, None, ln4, cr)
        experts = [(f1wg[e], f1wu[e], f1wd[e], ("col", wg1, e))
                   for e in range(E0)]
        experts.append((f1sg[:, 0:H1], f1su[:, 0:H1], f1sd[0:H1, :],
                        ("const", 0.5)))
        experts.append((f1sg[:, H1:HS], f1su[:, H1:HS], f1sd[H1:HS, :],
                        ("const", 0.5)))
        f1 = emit_ffn1(p, b, "a", experts, ln4 + gr1)
        if upto == "ffn1":
            stx = emit_store_x(p, b, x_out, f1)
            p.finalize(stack, stx)
            return nc
        ln5 = emit_ln(p, b, f1, cr)
        ag1 = emit_xnt_ag(p, b, "1", xl[1], xg[1], ln5)
        qk1, nM1, negM1 = emit_qkv_units(
            p, b, "1", False, wqkv1, bqkv1, xg[1].ap(), qkvT[1].ap(),
            None, None, ag1, f1)
        at1 = emit_att(p, b, "1", False, qkvT[1].ap(), ot[1].ap(), negM1,
                       qk1, nM1, cr)
        oag1 = emit_ot_ag(p, b, "1", ot[1], og[1], at1)
        pj1 = emit_proj(p, b, "1", og[1].ap(), wproj1, bproj1, ln5, cr, oag1)
        ln6 = emit_ln(p, b, pj1, cr)
        wg1b, gr1b = emit_gating(p, b, "ffn1", "b", rout_w, None, ln6, cr)
        experts2 = [(f1wg[e], f1wu[e], f1wd[e], ("col", wg1b, e))
                    for e in range(E0)]
        experts2.append((f1sg[:, 0:H1], f1su[:, 0:H1], f1sd[0:H1, :],
                         ("const", 0.5)))
        experts2.append((f1sg[:, H1:HS], f1su[:, H1:HS], f1sd[H1:HS, :],
                         ("const", 0.5)))
        f1b = emit_ffn1(p, b, "a", experts2, ln6 + gr1b)
        stx = emit_store_x(p, b, x_out, f1b)
        p.finalize(stack, stx)
    return nc



def _consts_host():
    c = np.zeros((128, 384), np.float32)
    c[:, 0:128] = np.eye(128)
    kk, qq = np.mgrid[0:128, 0:128]
    c[:, 128:256] = (kk <= qq).astype(np.float32)
    c[:, 256:384] = np.eye(128)
    return c


def _rope_tables():
    invf = 1.0 / (10000.0 ** (np.arange(0, 32, 2, dtype=np.float64) / 32))
    f = np.arange(SEQ, dtype=np.float64)[:, None] * invf[None, :]
    return (np.ascontiguousarray(np.cos(f).T.astype(np.float32)),
            np.ascontiguousarray(np.sin(f).T.astype(np.float32)))


_NC_CACHE = {}


def _get_nc(name, builder):
    if name not in _NC_CACHE:
        _NC_CACHE[name] = builder()
    return _NC_CACHE[name]


def kernel(L0_attn_hidden_states, L0_attn_W_qkv, L0_attn_b_qkv,
           L0_attn_W_dense, L0_attn_b_dense, L0_ffn_gate_w, L0_ffn_gate_b,
           L0_ffn_expert_w, L0_ffn_expert_b, L1_attn_w_qkv, L1_attn_b_qkv,
           L1_attn_w_proj, L1_attn_b_proj, L1_ffn_router_w, L1_ffn_wg,
           L1_ffn_wu, L1_ffn_wd, L1_ffn_sg, L1_ffn_su, L1_ffn_sd):
    consts = _consts_host()
    ct, st = _rope_tables()
    cores = list(range(8))
    f32 = lambda a: np.ascontiguousarray(np.asarray(a), dtype=np.float32)

    x = np.asarray(L0_attn_hidden_states)
    W0 = f32(L0_attn_W_qkv)
    b0 = f32(L0_attn_b_qkv)
    W1 = f32(L1_attn_w_qkv)
    b1 = f32(L1_attn_b_qkv)

    def unit_cols(layer, h):
        if layer == 0:
            return list(range(h * 384, (h + 1) * 384))
        return (list(range(h * 128, (h + 1) * 128)) +
                list(range(2048 + h * 128, 2048 + (h + 1) * 128)) +
                list(range(4096 + h * 128, 4096 + (h + 1) * 128)))

    nc = _get_nc("fused", build_fused)
    in_maps = []
    for c in cores:
        bb, cc = divmod(c, 4)
        heads = [4 * cc + i for i in range(NU)]
        cols0 = sum((unit_cols(0, h) for h in heads), [])
        cols1 = sum((unit_cols(1, h) for h in heads), [])
        in_maps.append({
            "x_in": np.ascontiguousarray(
                x[bb, cc * TOK:(cc + 1) * TOK].reshape(TB, 128, D)),
            "gate_wT": f32(np.asarray(L0_ffn_gate_w).T),
            "gate_b": f32(L0_ffn_gate_b).reshape(1, 8),
            "exp_w": f32(L0_ffn_expert_w),
            "exp_b": f32(L0_ffn_expert_b),
            "wqkv0": np.ascontiguousarray(W0[:, cols0]),
            "bqkv0": np.ascontiguousarray(b0[cols0]).reshape(1, -1),
            "wdense0": f32(L0_attn_W_dense),
            "bdense0": f32(L0_attn_b_dense).reshape(1, -1),
            "ct": ct, "st": st,
            "rout_w": f32(L1_ffn_router_w),
            "f1wg": f32(L1_ffn_wg), "f1wu": f32(L1_ffn_wu),
            "f1wd": f32(L1_ffn_wd),
            "f1sg": f32(L1_ffn_sg), "f1su": f32(L1_ffn_su),
            "f1sd": f32(L1_ffn_sd),
            "wqkv1": np.ascontiguousarray(W1[:, cols1]),
            "bqkv1": np.ascontiguousarray(b1[cols1]).reshape(1, -1),
            "wproj1": f32(L1_attn_w_proj),
            "bproj1": f32(L1_attn_b_proj).reshape(1, -1),
            "consts": consts,
        })
    res = run_bass_kernel_spmd(nc, in_maps, core_ids=cores)
    out = np.empty((B, SEQ, D), np.float32)
    for c in cores:
        bb, cc = divmod(c, 4)
        out[bb, cc * TOK:(cc + 1) * TOK] = \
            np.asarray(res.results[c]["x_out"]).reshape(TOK, D)
    return out


# version 04
# speedup vs baseline: 1.1875x; 1.0315x over previous
import math
from contextlib import ExitStack

import numpy as np
import ml_dtypes

import concourse.bass as bass
import concourse.mybir as mybir
from concourse.bass_utils import run_bass_kernel_spmd

F32 = mybir.dt.float32
BF16 = mybir.dt.bfloat16
AF = mybir.ActivationFunctionType
ALU = mybir.AluOpType
AX = mybir.AxisListType

B, SEQ, D = 2, 8192, 2048
TOK, TB, NDC = 2048, 16, 16
NH, HD = 16, 128
NU = 4
E0 = 8
H1, NHC1, HS = 1408, 11, 2816
SCALE = 1.0 / math.sqrt(HD)
BF = ml_dtypes.bfloat16
RG = [[0, 1, 2, 3], [4, 5, 6, 7]]

XNT_COLS = 32768
WORK_COLS = 24576
ARENA_COLS = XNT_COLS + WORK_COLS

ENGS = ("pe", "dv", "ac", "gp", "sp")


class S:
    __slots__ = ("name", "count")

    def __init__(self, name):
        self.name, self.count = name, 0


class Prog:
    def __init__(self, nc):
        self.nc = nc
        self.stream = {e: [] for e in ENGS}
        self.sems = {}
        self.hwm = {e: {} for e in ENGS}
        self.last_cc = None

    def sem(self, name):
        if name not in self.sems:
            self.sems[name] = S(name)
        return self.sems[name]

    def op(self, eng, method, *args, waits=(), inc=None, inc_n=1, **kwargs):
        filt = []
        for s, v in waits:
            if v is None or v <= 0:
                continue
            if v > self.hwm[eng].get(s.name, 0):
                filt.append((s, v))
                self.hwm[eng][s.name] = v
        self.stream[eng].append((method, args, kwargs, filt, inc, inc_n))
        if inc is not None:
            inc.count += inc_n
            return inc.count
        return None

    def dma(self, eng, out, in_, sem, waits=(), **kw):
        return self.op(eng, "dma_start", waits=waits, inc=sem, inc_n=16,
                       out=out, in_=in_, **kw)

    def replay(self, engname, engine, handles):
        for method, args, kwargs, waits, inc, inc_n in self.stream[engname]:
            for s, v in waits:
                engine.wait_ge(handles[s.name], v)
            if callable(method):
                if getattr(method, "wants_handles", False):
                    inst = method(engine, handles)
                else:
                    inst = method(engine)
            else:
                inst = getattr(engine, method)(*args, **kwargs)
            if inc is not None:
                assert inst is not None
                inst.then_inc(handles[inc.name], inc_n)

    def finalize(self, stack, final_waits):
        nc = self.nc
        handles = {name: stack.enter_context(nc.semaphore(name))
                   for name in self.sems}
        block = stack.enter_context(nc.Block())
        p, fw = self, list(final_waits)

        @block.tensor
        def _(t):
            p.replay("pe", t, handles)

        @block.vector
        def _(v):
            p.replay("dv", v, handles)

        @block.scalar
        def _(s):
            p.replay("ac", s, handles)

        @block.gpsimd
        def _(g):
            p.replay("gp", g, handles)

        @block.sync
        def _(sy):
            p.replay("sp", sy, handles)
            for s, v in fw:
                sy.wait_ge(handles[s.name], v)


class Bufs:

    def __init__(self, nc, stack):
        self.nc, self.stack, self.t = nc, stack, {}
        self._ar = None

    def sb(self, name, shape, dtype):
        if name not in self.t:
            self.t[name] = self.stack.enter_context(
                self.nc.sbuf_tensor(name, shape, dtype))
        return self.t[name]

    def ps(self, name, shape, dtype):
        if name not in self.t:
            self.t[name] = self.stack.enter_context(
                self.nc.psum_tensor(name, shape, dtype))
        return self.t[name]

    def arena(self, c0, ncols, nrow=128, f32=False):
        if self._ar is None:
            self._ar = self.stack.enter_context(
                self.nc.sbuf_tensor("arena", [128, ARENA_COLS], BF16))
        v = self._ar[0:nrow, c0:c0 + ncols]
        if f32:
            v = v.bitcast(F32)
        return v

    def work(self, c0, ncols, nrow=128, f32=False):
        return self.arena(XNT_COLS + c0, ncols, nrow=nrow, f32=f32)



def register_consts(p, b):
    cz = b.sb("constz", [128, 2], F32)
    s_gp = p.sem("s_gp")
    p.op("gp", "memset", cz[:, 0:1], 0.0, inc=s_gp)
    p.op("gp", "memset", cz[:, 1:2], 1e-5, inc=s_gp)
    b.nc.const_aps.register(F32, 0.0, cz[:, 0:1])
    b.nc.const_aps.register(F32, 1e-5, cz[:, 1:2])


def alloc_common(p, b):
    for i in range(4):
        b.ps(f"ps{i}", [128, 512], F32)
    b.ps("ps4", [128, 512], F32)
    b.ps("ps5", [128, 512], F32)
    b.ps("psE", [128, 512], F32)
    b.ps("psF", [128, 512], F32)
    b.t["ptrA"] = b.t["psE"][:].bitcast(BF16)
    b.t["ptrB"] = b.t["psF"][:].bitcast(BF16)


def emit_consts(p, b, consts):
    ones1 = b.sb("ones1", [1, 512], BF16)
    ones128 = b.sb("ones128", [128, 1], BF16)
    idb = b.sb("idb", [128, 128], BF16)
    idf = b.sb("idf", [128, 128], F32)
    trib = b.sb("trib", [128, 128], BF16)
    s_gp, s_ld = p.sem("s_gp"), p.sem("s_ldc")
    register_consts(p, b)
    p.op("gp", "memset", ones1[:], 1.0, inc=s_gp)
    p.op("gp", "memset", ones128[:], 1.0, inc=s_gp)
    p.dma("gp", idb[:], consts[:, 0:128], s_ld)
    p.dma("gp", trib[:], consts[:, 128:256], s_ld)
    p.dma("sp", idf[:], consts[:, 256:384], s_ld)
    return [(s_ld, s_ld.count), (s_gp, s_gp.count)]


def emit_load_x(p, b, x_ap):
    x_sb = b.sb("x_sb", [128, TB * D], BF16)
    s = p.sem("s_ldx")
    for tb in range(TB):
        p.dma("gp", x_sb[:, tb * D:(tb + 1) * D], x_ap[tb], s)
    return [(s, s.count)]


def emit_store_x(p, b, out_ap, waits):
    s = p.sem("s_stx")
    x_sb = b.t["x_sb"]
    for tb in range(TB):
        p.dma("gp", out_ap[tb], x_sb[:, tb * D:(tb + 1) * D], s, waits=waits)
    return [(s, s.count)]


def emit_ln(p, b, x_ready, consts_ready):
    x_sb = b.t["x_sb"]
    xnT = b.arena(0, NDC * TOK)
    b.t["xnT"] = xnT
    xn_st = b.work(16384, 2 * D)
    bnst = b.sb("bnst", [128, TB * 24], F32)
    mv = b.sb("mv", [128, TB * 2], F32)
    stat = b.sb("stat", [128, 2 * TB], F32)
    ptrA, ptrB = b.t["ptrA"], b.t["ptrB"]
    s_dv, s_ac, s_pe = p.sem("s_dv"), p.sem("s_ac"), p.sem("s_pe")

    t_stats = None
    for tb in range(TB):
        for ch in range(4):
            t_stats = p.op("dv", "bn_stats",
                           bnst[:, tb * 24 + ch * 6: tb * 24 + (ch + 1) * 6],
                           x_sb[:, tb * D + ch * 512: tb * D + (ch + 1) * 512],
                           waits=x_ready,
                           inc=s_dv if (tb, ch) == (TB - 1, 3) else None)
    t_aggr = None
    for tb in range(TB):
        t_aggr = p.op("dv", "bn_aggr", mv[:, tb * 2: tb * 2 + 2],
                      bnst[:, tb * 24:(tb + 1) * 24],
                      waits=[(s_dv, t_stats)],
                      inc=s_dv if tb == TB - 1 else None)
    varv = mv[:].rearrange("p (t two) -> p t two", two=2)[:, :, 1:2]
    t_sqrt = p.op("ac", "activation", stat[:, 0:TB], varv, AF.Sqrt, bias=1e-5,
                  waits=[(s_dv, t_aggr)], inc=s_ac)
    t_rstd = p.op("dv", "reciprocal", stat[:, TB:2 * TB], stat[:, 0:TB],
                  waits=[(s_ac, t_sqrt)], inc=s_dv)

    tp_done, ev_done = {}, {}
    for tb in range(TB):
        par = tb % 2
        w = [(s_dv, t_rstd)] + x_ready
        if tb >= 2:
            w.append((s_pe, tp_done[(tb - 2, 1)]))
        t_app = p.op("dv", "tensor_scalar", xn_st[:, par * D:(par + 1) * D],
                     x_sb[:, tb * D:(tb + 1) * D],
                     mv[:, tb * 2: tb * 2 + 1], stat[:, TB + tb: TB + tb + 1],
                     ALU.subtract, ALU.mult, waits=w, inc=s_dv)
        for grp in range(2):
            bank = ptrA if grp == 0 else ptrB
            wts = [(s_dv, t_app)] + consts_ready
            if (tb - 1, grp) in ev_done:
                wts.append((s_dv, ev_done[(tb - 1, grp)]))
            tv = None
            for k in range(8):
                dc = grp * 8 + k
                tv = p.op("pe", "transpose", bank[:, k * 128:(k + 1) * 128],
                          xn_st[:, par * D + dc * 128: par * D + (dc + 1) * 128],
                          b.t["idb"][:], waits=wts,
                          inc=s_pe if k == 7 else None)
            tp_done[(tb, grp)] = tv
            ev_done[(tb, grp)] = p.op(
                "dv", "tensor_copy",
                xnT[:].rearrange("p (dc t) -> p dc t", dc=NDC)[
                    :, grp * 8:(grp + 1) * 8, tb * 128:(tb + 1) * 128],
                bank[:].rearrange("p (k t) -> p k t", k=8),
                waits=[(s_pe, tv)], inc=s_dv)
    return [(s_dv, p.sems["s_dv"].count)]


def xnT_mv(b, dc, t0, n):
    return b.t["xnT"][:, dc * TOK + t0: dc * TOK + t0 + n]


def xnT_st(b, dc, tb):
    return b.t["xnT"][:, dc * TOK + tb * 128: dc * TOK + (tb + 1) * 128]


def emit_gating(p, b, kind, tag, gate_wT_ap, gate_b_ap, xnT_ready,
                consts_ready):
    gwr = b.work(0, D, nrow=8)
    lgT = b.work(2048, 4096, nrow=8, f32=True)
    gwT = b.sb("gwT", [128, NDC * 8], BF16)
    gb = b.sb("gb", [1, 8], BF16)
    lg = b.sb("lg", [128, TB * 8], F32)
    top8 = b.sb("top8", [128, TB * 8], F32)
    wg = b.sb("wgate" + tag, [128, TB * 8], F32)
    ps_g, ps_t = b.t["ps4"], b.t["ps5"]
    ptrA = b.t["ptrA"]
    s_ld = p.sem("s_ldg" + tag)
    s_pe, s_dv, s_ac = p.sem("s_pe"), p.sem("s_dv"), p.sem("s_ac")

    p.dma("gp", gwr, gate_wT_ap, s_ld, waits=consts_ready + xnT_ready)
    if gate_b_ap is not None:
        p.dma("gp", gb[:], gate_b_ap, s_ld)
    v_w = s_ld.count

    for grp in range(2):
        tv = None
        for k in range(8):
            dc = grp * 8 + k
            tv = p.op("pe", "transpose", ptrA[:, k * 128: k * 128 + 8],
                      gwr[:, dc * 128:(dc + 1) * 128], b.t["idb"][0:8, 0:8],
                      waits=[(s_ld, v_w)] + xnT_ready +
                      ([(s_dv, p.sems["s_dv"].count)] if grp else []),
                      inc=s_pe if k == 7 else None)
        p.op("dv", "tensor_copy",
             gwT[:].rearrange("p (dc e) -> p dc e", e=8)[:, grp * 8:(grp + 1) * 8, :],
             ptrA[:].rearrange("p (k t) -> p k t", k=8)[:, :, 0:8],
             waits=[(s_pe, tv)], inc=s_dv)
    t_gwT = p.sems["s_dv"].count

    prev_ev = None
    for tc in range(4):
        def mmfn(t, tc=tc):
            last = None
            if gate_b_ap is not None:
                last = t.matmul(ps_g[0:8, 0:512], gb[0:1, :],
                                b.t["ones1"][0:1, 0:512], start=True, stop=False)
            for dc in range(NDC):
                last = t.matmul(ps_g[0:8, 0:512], gwT[:, dc * 8:(dc + 1) * 8],
                                xnT_mv(b, dc, tc * 512, 512),
                                start=(dc == 0 and gate_b_ap is None),
                                stop=(dc == NDC - 1))
            return last
        w = xnT_ready + [(s_dv, t_gwT)]
        if prev_ev is not None:
            w.append((s_ac, prev_ev))
        tv = p.op("pe", mmfn, waits=w, inc=s_pe)
        prev_ev = p.op("ac", "copy", lgT[:, tc * 512:(tc + 1) * 512],
                       ps_g[0:8, 0:512], waits=[(s_pe, tv)], inc=s_ac)

    tv = None
    for tb in range(TB):
        tv = p.op("pe", "transpose", ps_t[:, tb * 8: tb * 8 + 8],
                  lgT[:, tb * 128:(tb + 1) * 128], b.t["idf"][0:8, 0:8],
                  waits=[(s_ac, prev_ev)], inc=s_pe if tb == TB - 1 else None)
    t_lg = p.op("dv", "tensor_copy", lg[:], ps_t[:, 0:TB * 8],
                waits=[(s_pe, tv)], inc=s_dv)

    if kind == "ffn0":
        t_m = None
        for tb in range(TB):
            t_m = p.op("dv", "max", top8[:, tb * 8:(tb + 1) * 8],
                       lg[:, tb * 8:(tb + 1) * 8], waits=[(s_dv, t_lg)],
                       inc=s_dv if tb == TB - 1 else None)
        for tb in range(TB):
            p.op("dv", "tensor_scalar", wg[:, tb * 8:(tb + 1) * 8],
                 lg[:, tb * 8:(tb + 1) * 8],
                 top8[:, tb * 8 + 1: tb * 8 + 2], 0.25, ALU.is_ge, ALU.mult,
                 waits=[(s_dv, t_m)], inc=s_dv if tb == TB - 1 else None)
    else:
        nm = b.sb("g1nm", [128, TB], F32)
        ex = b.sb("g1ex", [128, TB * 8], F32)
        lsum = b.sb("g1ls", [128, TB], F32)
        rl = b.sb("g1rl", [128, TB], F32)
        pv = b.sb("g1pv", [128, TB * 8], F32)
        den = b.sb("g1den", [128, TB], F32)
        t1 = None
        for tb in range(TB):
            t1 = p.op("dv", "tensor_reduce", nm[:, tb: tb + 1],
                      lg[:, tb * 8:(tb + 1) * 8], AX.X, ALU.max, negate=True,
                      waits=[(s_dv, t_lg)], inc=s_dv if tb == TB - 1 else None)
        t2 = None
        for tb in range(TB):
            t2 = p.op("ac", "activation", ex[:, tb * 8:(tb + 1) * 8],
                      lg[:, tb * 8:(tb + 1) * 8], AF.Exp,
                      bias=nm[:, tb: tb + 1], accum_out=lsum[:, tb: tb + 1],
                      waits=[(s_dv, t1)], inc=s_ac if tb == TB - 1 else None)
        t3 = p.op("dv", "reciprocal", rl[:], lsum[:], waits=[(s_ac, t2)],
                  inc=s_dv)
        t4 = None
        for tb in range(TB):
            t4 = p.op("dv", "tensor_scalar_mul", pv[:, tb * 8:(tb + 1) * 8],
                      ex[:, tb * 8:(tb + 1) * 8], rl[:, tb: tb + 1],
                      waits=[(s_dv, t3)], inc=s_dv if tb == TB - 1 else None)
        t5 = None
        for tb in range(TB):
            t5 = p.op("dv", "max", top8[:, tb * 8:(tb + 1) * 8],
                      pv[:, tb * 8:(tb + 1) * 8], waits=[(s_dv, t4)],
                      inc=s_dv if tb == TB - 1 else None)
        t6 = None
        for tb in range(TB):
            t6 = p.op("dv", "tensor_scalar", den[:, tb: tb + 1],
                      top8[:, tb * 8: tb * 8 + 1],
                      top8[:, tb * 8 + 1: tb * 8 + 2], 1e-20,
                      ALU.add, ALU.add, waits=[(s_dv, t5)],
                      inc=s_dv if tb == TB - 1 else None)
        t7 = p.op("dv", "reciprocal", den[:], den[:], waits=[(s_dv, t6)],
                  inc=s_dv)
        t7b = p.op("dv", "tensor_scalar_mul", den[:], den[:], 0.5,
                   waits=[(s_dv, t7)], inc=s_dv)
        t8 = None
        for tb in range(TB):
            t8 = p.op("dv", "tensor_scalar", wg[:, tb * 8:(tb + 1) * 8],
                      pv[:, tb * 8:(tb + 1) * 8],
                      top8[:, tb * 8 + 1: tb * 8 + 2], None, ALU.is_ge,
                      waits=[(s_dv, t7b)], inc=s_dv if tb == TB - 1 else None)
        for tb in range(TB):
            p.op("dv", "scalar_tensor_tensor", wg[:, tb * 8:(tb + 1) * 8],
                 pv[:, tb * 8:(tb + 1) * 8], den[:, tb: tb + 1],
                 wg[:, tb * 8:(tb + 1) * 8], ALU.mult, ALU.mult,
                 waits=[(s_dv, t8)], inc=s_dv if tb == TB - 1 else None)
    return wg, [(s_dv, p.sems["s_dv"].count)]


def emit_ffn0(p, b, tag, expert_w_ap, expert_b_ap, wg, gate_ready, xnT_ready):
    x_sb = b.t["x_sb"]
    wbuf = [b.work(0, NDC * 512), b.work(NDC * 512, NDC * 512)]
    be = b.work(16384, D, nrow=1)
    tmp = [b.sb(f"f0t{i}", [128, 512], BF16) for i in range(2)]
    ps = [b.t[f"ps{i}"] for i in range(4)]
    s_w = [p.sem(f"s_f0w{tag}0"), p.sem(f"s_f0w{tag}1")]
    s_b = p.sem(f"s_f0b{tag}")
    s_pe, s_dv, s_ac = p.sem("s_pe"), p.sem("s_dv"), p.sem("s_ac")

    mm_done, ev_done = {}, {}
    idx = 0
    for e in range(E0):
        bw = [(s_pe, mm_done[(e - 1, 3)])] if e else (gate_ready + xnT_ready)
        bd = p.dma("gp", be, expert_b_ap[e: e + 1, :], s_b, waits=bw)
        for oc in range(4):
            par = (e * 4 + oc) % 2
            prev = (e, oc - 2) if oc >= 2 else (e - 1, oc + 2)
            w = [(s_b, bd)]
            if prev in mm_done:
                w.append((s_pe, mm_done[prev]))
            wd = p.dma("gp", wbuf[par].rearrange("p (dc o) -> p dc o", o=512),
                       expert_w_ap[e][:, oc * 512:(oc + 1) * 512]
                       .rearrange("(dc p) o -> p dc o", p=128),
                       s_w[par], waits=w)
            for tb in range(TB):
                bank = ps[idx % 4]
                w = xnT_ready + [(s_w[par], wd)]
                if idx - 4 in ev_done:
                    w.append((s_dv, ev_done[idx - 4]))

                def mmfn(t, oc=oc, par=par, bank=bank, tb=tb):
                    t.matmul(bank[:], b.t["ones1"][0:1, 0:128],
                             be[0:1, oc * 512:(oc + 1) * 512],
                             start=True, stop=False)
                    last = None
                    for dc in range(NDC):
                        last = t.matmul(bank[:], xnT_st(b, dc, tb),
                                        wbuf[par][:, dc * 512:(dc + 1) * 512],
                                        start=False, stop=(dc == NDC - 1))
                    return last
                tv = p.op("pe", mmfn, waits=w, inc=s_pe)
                if tb == TB - 1:
                    mm_done[(e, oc)] = tv
                slot = idx % 2
                w2 = [(s_pe, tv)] + gate_ready
                if idx - 2 in ev_done:
                    w2.append((s_dv, ev_done[idx - 2]))
                ta = p.op("ac", "activation", tmp[slot][:], bank[:], AF.Relu,
                          scale=wg[:, tb * 8 + e: tb * 8 + e + 1],
                          waits=w2, inc=s_ac)
                xs = x_sb[:, tb * D + oc * 512: tb * D + (oc + 1) * 512]
                ev_done[idx] = p.op("dv", "tensor_add", xs, xs, tmp[slot][:],
                                    waits=[(s_ac, ta)], inc=s_dv)
                idx += 1
    return [(s_dv, p.sems["s_dv"].count)]


def emit_ffn1(p, b, tag, experts, xnT_ready):
    x_sb = b.t["x_sb"]
    gbuf = [b.work(i * 2048, 2048) for i in range(2)]
    ubuf = [b.work(4096 + i * 2048, 2048) for i in range(2)]
    dbuf = [b.work(8192 + i * 2048, 2048) for i in range(4)]
    hsb = b.work(16384, 4 * 2048)
    sil = [b.sb(f"f1s{i}", [128, 512], BF16) for i in range(2)]
    ps = [b.t[f"ps{i}"] for i in range(4)]
    s_g = [p.sem(f"s_f1g{tag}0"), p.sem(f"s_f1g{tag}1")]
    s_u = [p.sem(f"s_f1u{tag}0"), p.sem(f"s_f1u{tag}1")]
    s_d = [p.sem(f"s_f1d{tag}{i}") for i in range(4)]
    s_pe, s_dv, s_ac = p.sem("s_pe"), p.sem("s_dv"), p.sem("s_ac")

    groups = [list(range(0, 4)), list(range(4, 8)), list(range(8, NHC1))]
    prev_down_mm = None
    down_ev = None
    comb = {}
    gu_last = {}
    for ei, (wg_ap, wu_ap, wd_ap, gate) in enumerate(experts):
        for gi, grp in enumerate(groups):
            nhc = len(grp)
            comb.clear()
            gu_last.clear()
            for li, hc in enumerate(grp):
                par = li % 2
                w = list(xnT_ready)
                if down_ev is not None:
                    w.append((s_dv, down_ev))
                if prev_down_mm is not None:
                    w.append((s_pe, prev_down_mm))
                if (par, "last") in gu_last:
                    w.append((s_dv, gu_last[(par, "last")]))
                gd = p.dma("gp", gbuf[par].rearrange("p (dc h) -> p dc h", h=128),
                           wg_ap[:, hc * 128:(hc + 1) * 128]
                           .rearrange("(dc p) h -> p dc h", p=128),
                           s_g[par], waits=w)
                ud = p.dma("gp", ubuf[par].rearrange("p (dc h) -> p dc h", h=128),
                           wu_ap[:, hc * 128:(hc + 1) * 128]
                           .rearrange("(dc p) h -> p dc h", p=128),
                           s_u[par], waits=w)
                for tc in range(4):
                    bA = ps[(tc % 2) * 2]
                    bU = ps[(tc % 2) * 2 + 1]
                    w2 = xnT_ready + [(s_g[par], gd), (s_u[par], ud)]
                    guard = (li, tc - 2) if tc >= 2 else (li - 1, tc + 2)
                    if guard in comb:
                        w2.append((s_dv, comb[guard]))
                    if down_ev is not None:
                        w2.append((s_dv, down_ev))
                    if prev_down_mm is not None:
                        w2.append((s_pe, prev_down_mm))

                    def gmm(t, par=par, tc=tc, bA=bA):
                        last = None
                        for dc in range(NDC):
                            last = t.matmul(
                                bA[:], gbuf[par][:, dc * 128:(dc + 1) * 128],
                                xnT_mv(b, dc, tc * 512, 512),
                                start=(dc == 0), stop=(dc == NDC - 1))
                        return last

                    def umm(t, par=par, tc=tc, bU=bU):
                        last = None
                        for dc in range(NDC):
                            last = t.matmul(
                                bU[:], ubuf[par][:, dc * 128:(dc + 1) * 128],
                                xnT_mv(b, dc, tc * 512, 512),
                                start=(dc == 0), stop=(dc == NDC - 1))
                        return last
                    tg = p.op("pe", gmm, waits=w2, inc=s_pe)
                    tu = p.op("pe", umm, waits=w2, inc=s_pe)
                    slot = tc % 2
                    w3 = [(s_pe, tg)]
                    prevc = comb.get((li, tc - 2), comb.get((li - 1, tc + 2)))
                    if prevc is not None:
                        w3.append((s_dv, prevc))
                    tsil = p.op("ac", "activation", sil[slot][:], bA[:],
                                AF.Silu, waits=w3, inc=s_ac)
                    cc = p.op("dv", "tensor_mul",
                              hsb[:, li * TOK + tc * 512: li * TOK + (tc + 1) * 512],
                              sil[slot][:], bU[:],
                              waits=[(s_pe, tu), (s_ac, tsil)], inc=s_dv)
                    comb[(li, tc)] = cc
                gu_last[(par, "last")] = comb[(li, 3)]
            comb_all = p.sems["s_dv"].count
            dld = []
            for li, hc in enumerate(grp):
                w = [(s_pe, prev_down_mm)] if prev_down_mm else []
                dld.append(p.dma("gp", dbuf[li],
                                 wd_ap[hc * 128:(hc + 1) * 128, :],
                                 s_d[li], waits=w))
            last_mm = None
            prev_ev = down_ev
            for tb in range(TB):
                w = [(s_dv, comb_all)] + \
                    [(s_d[li], dld[li]) for li in range(nhc)]
                if prev_ev is not None:
                    w.append((s_dv, prev_ev))

                def dmm(t, tb=tb, nhc=nhc):
                    last = None
                    for li in range(nhc):
                        for dcol in range(4):
                            last = t.matmul(
                                ps[dcol][:],
                                hsb[:, li * TOK + tb * 128: li * TOK + (tb + 1) * 128],
                                dbuf[li][:, dcol * 512:(dcol + 1) * 512],
                                start=(li == 0), stop=(li == nhc - 1))
                    return last
                last_mm = p.op("pe", dmm, waits=w, inc=s_pe)
                for dcol in range(4):
                    xs = x_sb[:, tb * D + dcol * 512: tb * D + (dcol + 1) * 512]
                    if gate[0] == "col":
                        sc_ = gate[1][:, tb * 8 + gate[2]: tb * 8 + gate[2] + 1]
                    else:
                        sc_ = gate[1]
                    prev_ev = p.op("dv", "scalar_tensor_tensor", xs,
                                   ps[dcol][:], sc_, xs, ALU.mult, ALU.add,
                                   waits=[(s_pe, last_mm)], inc=s_dv)
            prev_down_mm = last_mm
            down_ev = prev_ev
    return [(s_dv, p.sems["s_dv"].count)]


def emit_xnt_ag(p, b, tag, xlc_ts, xgc_ts, xnT_ready):
    s_st = p.sem("s_xst" + tag)
    s_cc = p.sem("s_cc" + tag)
    xnT = b.t["xnT"]
    cc_marks = []
    for cp in range(4):
        last = None
        for dc in range(NDC):
            last = p.dma("sp", xlc_ts[cp].ap()[dc * 128:(dc + 1) * 128, :],
                         xnT[:, dc * TOK + cp * 512: dc * TOK + (cp + 1) * 512],
                         s_st, waits=xnT_ready)

        def fn(g, cp=cp):
            return g.collective_compute(
                "AllGather", ALU.bypass, replica_groups=RG,
                ins=[xlc_ts[cp].ap().opt()], outs=[xgc_ts[cp].ap().opt()])
        w = [(s_st, last)] + ([p.last_cc] if p.last_cc else [])
        cc_marks.append(p.op("gp", fn, waits=w, inc=s_cc))
        p.last_cc = (s_cc, cc_marks[-1])
    return s_cc, cc_marks, (s_st, p.sems["s_xst" + tag].count)


def emit_qkv_units(p, b, tag, rope, wq_ap, bq_ap, xgc_aps, qkvT_ap,
                   ct_ap, st_ap, ag_info, region_free):
    warr = b.work(0, 12 * NDC * 128)
    cb = [b.arena(0, NDC * 512), b.arena(NDC * 512, NDC * 512)]
    stg = [b.arena(16384 + i * 512, 512) for i in range(4)]
    sqt = [b.arena(18432 + i * 512, 512) for i in range(2)]
    bqt = b.arena(19456, 12 * 128, nrow=1)
    nacc = b.sb("nacc" + tag, [1, 12], F32)
    ntmp = b.sb("ntmp" + tag, [1, 1], F32)
    negm = b.sb("negm" + tag, [1, NU], F32)
    negmb = b.sb("negmb" + tag, [1, NU], BF16)
    negM = b.sb("negM" + tag, [128, NU], F32)
    ps = [b.t[f"ps{i}"] for i in range(4)]
    ps_n = [b.t["ps4"], b.t["ps5"]]
    s_w = p.sem("s_uw" + tag)
    s_cx = p.sem("s_ucx" + tag)
    s_st = p.sem("s_ust" + tag)
    s_pe, s_dv, s_ac = p.sem("s_pe"), p.sem("s_dv"), p.sem("s_ac")
    s_gp = p.sem("s_gp")
    s_agcc, ag_marks, st_flush = ag_info

    wcnt = {}
    for fb in range(12):
        for dc in range(NDC):
            wcnt[fb] = p.dma(
                "gp", warr[:, (fb * NDC + dc) * 128:(fb * NDC + dc + 1) * 128],
                wq_ap[dc * 128:(dc + 1) * 128, fb * 128:(fb + 1) * 128],
                s_w, waits=region_free if (fb == 0 and dc == 0) else ())
    bload = p.dma("gp", bqt, bq_ap, s_w, waits=[st_flush])

    t_z = p.op("gp", "memset", nacc[:], 0.0, inc=s_gp)

    mm_marks = {}
    ev_marks = {}
    st_marks = {}
    sq_marks = {}
    tn_marks = {}
    tr_marks = {}
    nred_prev = None
    nrm_pend = None
    idx = 0
    nqk = 0

    def emit_norm(pfb, pnqk):
        nonlocal nred_prev
        nb = ps_n[pnqk % 2]
        wtn = [(s_ac, sq_marks[pnqk])]
        if pnqk - 2 in tr_marks:
            wtn.append((s_dv, tr_marks[pnqk - 2]))
        tn = p.op("pe", lambda t, nb=nb, pnqk=pnqk: t.matmul(
            nb[0:1, 0:512], b.t["ones128"][:], sqt[pnqk % 2][:],
            start=True, stop=True),
            waits=wtn, inc=s_pe)
        tn_marks[pnqk] = tn
        w = [(s_pe, tn), (s_gp, t_z)]
        if nred_prev is not None:
            w.append((s_dv, nred_prev))
        tr = p.op("dv", "tensor_reduce", ntmp[:, 0:1],
                  nb[0:1, 0:512], AX.X, ALU.max, waits=w, inc=s_dv)
        tr_marks[pnqk] = tr
        nred_prev = p.op("dv", "tensor_max", nacc[:, pfb:pfb + 1],
                         nacc[:, pfb:pfb + 1], ntmp[:, 0:1],
                         waits=[(s_dv, tr)], inc=s_dv)

    prev_ch = {}
    for ci, (cp, r) in enumerate([(cp, r) for cp in range(4)
                                  for r in range(4)]):
        ch = r * 4 + cp
        par = ci % 2
        w = [(s_agcc, ag_marks[cp]), st_flush] + region_free
        if ci >= 2:
            w.append((s_pe, mm_marks[(prev_ch[ci - 2], 11)]))
        prev_ch[ci] = ch
        ld = None
        for dc in range(NDC):
            ld = p.dma("sp", cb[par][:, dc * 512:(dc + 1) * 512],
                       xgc_aps[cp][r * D + dc * 128: r * D + (dc + 1) * 128, :],
                       s_cx, waits=w if dc == 0 else ())
        for fb in range(12):
            bank = ps[idx % 4]
            w2 = [(s_cx, ld), (s_w, max(wcnt[fb], bload))]
            if idx - 4 in ev_marks:
                w2.append((s_ac, ev_marks[idx - 4]))

            def mmfn(t, fb=fb, par=par, bank=bank):
                t.matmul(bank[:], bqt[0:1, fb * 128:(fb + 1) * 128],
                         b.t["ones1"][0:1, 0:512], start=True, stop=False)
                last = None
                for dc in range(NDC):
                    last = t.matmul(
                        bank[:],
                        warr[:, (fb * NDC + dc) * 128:(fb * NDC + dc + 1) * 128],
                        cb[par][:, dc * 512:(dc + 1) * 512],
                        start=False, stop=(dc == NDC - 1))
                return last
            tv = p.op("pe", mmfn, waits=w2, inc=s_pe)
            mm_marks[(ch, fb)] = tv
            if nrm_pend is not None:
                emit_norm(*nrm_pend)
                nrm_pend = None
            slot = idx % 4
            w3 = [(s_pe, tv)]
            if idx - 4 in st_marks:
                w3.append((s_st, st_marks[idx - 4]))
            ev = p.op("ac", "copy", stg[slot][:], bank[:], waits=w3, inc=s_ac)
            ev_marks[idx] = ev
            st_marks[idx] = p.dma(
                "sp", qkvT_ap[fb * 128:(fb + 1) * 128,
                              ch * 512:(ch + 1) * 512],
                stg[slot][:], s_st, waits=[(s_ac, ev)])
            if fb % 3 != 2:
                wsq = [(s_ac, ev)]
                if nqk - 2 in tn_marks:
                    wsq.append((s_pe, tn_marks[nqk - 2]))
                sq = p.op("ac", "activation", sqt[nqk % 2][:],
                          stg[slot][:], AF.Square, waits=wsq, inc=s_ac)
                sq_marks[nqk] = sq
                nrm_pend = (fb, nqk)
                nqk += 1
            idx += 1
    if nrm_pend is not None:
        emit_norm(*nrm_pend)
        nrm_pend = None

    st_all = p.sems["s_ust" + tag].count
    sq_all = p.sems["s_ac"].count
    nrm_all = p.sems["s_dv"].count

    naccv = nacc[:].rearrange("p (u t) -> p u t", t=3)
    t_pr = p.op("dv", "tensor_mul",
                negm[:].rearrange("p (u o) -> p u o", o=1),
                naccv[:, :, 0:1], naccv[:, :, 1:2],
                waits=[(s_dv, nrm_all)], inc=s_dv)
    t_sq = p.op("ac", "activation", negm[:], negm[:], AF.Sqrt, bias=0.0,
                waits=[(s_dv, t_pr)], inc=s_ac)
    t_sc = p.op("dv", "tensor_scalar", negm[:], negm[:],
                -1.05 * SCALE, -1.0, ALU.mult, ALU.add,
                waits=[(s_ac, t_sq)], inc=s_dv)
    t_cb = p.op("dv", "tensor_copy", negmb[:], negm[:], waits=[(s_dv, t_sc)],
                inc=s_dv)
    t_bc = p.op("pe", lambda t: t.matmul(
        ps_n[0][:, 0:NU], b.t["ones1"][0:1, 0:128], negmb[0:1, :],
        start=True, stop=True), waits=[(s_dv, t_cb)], inc=s_pe)
    t_M = p.op("dv", "tensor_copy", negM[:], ps_n[0][:, 0:NU],
               waits=[(s_pe, t_bc)], inc=s_dv)
    negM_ready = [(s_dv, t_M)]

    if not rope:
        return [[(s_st, st_all)]] * NU, negM_ready, negM

    ctt = [b.arena(s * 16384, 4096, nrow=16, f32=True) for s in range(2)]
    stt = [b.arena(s * 16384 + 4096, 4096, nrow=16, f32=True)
           for s in range(2)]
    q0 = [b.arena(s * 16384 + 8192, 2048, nrow=16) for s in range(2)]
    q1 = [b.arena(s * 16384 + 10240, 2048, nrow=16) for s in range(2)]
    rA = [b.arena(s * 16384 + 12288, 2048, nrow=16) for s in range(2)]
    rB = [b.arena(s * 16384 + 14336, 2048, nrow=16) for s in range(2)]
    s_rl = p.sem("s_url" + tag)
    s_rs = p.sem("s_urs" + tag)
    afree = [(s_pe, mm_marks[(15, 11)]), (s_st, st_all), (s_ac, sq_all)]
    m6_hist = {}
    sb_hist = {}
    unit_rope = []
    it = 0
    for u in range(NU):
        for qk in range(2):
            fb = 3 * u + qk
            row = fb * 128
            for qtr in range(4):
                h0 = qtr * 2048
                s = it % 2
                w = list(afree)
                if it - 2 in m6_hist:
                    w.append((s_dv, m6_hist[it - 2]))
                    w.append((s_rs, sb_hist[it - 2]))
                tld = p.dma("sp", ctt[s], ct_ap[:, h0:h0 + 2048], s_rl,
                            waits=w)
                tld = p.dma("sp", stt[s], st_ap[:, h0:h0 + 2048], s_rl)
                l1 = p.dma("sp", q0[s][:],
                           qkvT_ap[row:row + 16, h0:h0 + 2048], s_rl)
                l1 = p.dma("sp", q1[s][:],
                           qkvT_ap[row + 16:row + 32, h0:h0 + 2048], s_rl)
                lw = [(s_rl, l1)]
                if it - 2 in sb_hist:
                    lw.append((s_rs, sb_hist[it - 2]))
                m1 = p.op("dv", "tensor_mul", rA[s][:], q0[s][:], ctt[s],
                          waits=lw, inc=s_dv)
                m2 = p.op("dv", "tensor_mul", rB[s][:], q1[s][:], stt[s],
                          waits=lw, inc=s_dv)
                m3 = p.op("dv", "tensor_sub", rA[s][:], rA[s][:], rB[s][:],
                          waits=[(s_dv, m2)], inc=s_dv)
                sA = p.dma("sp", qkvT_ap[row:row + 16, h0:h0 + 2048],
                           rA[s][:], s_rs, waits=[(s_dv, m3)])
                m4 = p.op("dv", "tensor_mul", rB[s][:], q1[s][:], ctt[s],
                          waits=[(s_dv, m3)], inc=s_dv)
                m5 = p.op("dv", "tensor_mul", rA[s][:], q0[s][:], stt[s],
                          waits=[(s_rs, sA), (s_dv, m4)], inc=s_dv)
                m6 = p.op("dv", "tensor_add", rB[s][:], rB[s][:], rA[s][:],
                          waits=[(s_dv, m5)], inc=s_dv)
                sb_hist[it] = p.dma(
                    "sp", qkvT_ap[row + 16:row + 32, h0:h0 + 2048],
                    rB[s][:], s_rs, waits=[(s_dv, m6)])
                m6_hist[it] = m6
                it += 1
    rope_done = [(s_st, st_all), (s_rs, p.sems["s_urs" + tag].count),
                 (s_dv, p.sems["s_dv"].count)]
    return [rope_done] * NU, negM_ready, negM


def emit_att(p, b, tag, causal, qkvT_ap, ot_ts, og_ts, negM, unit_ready,
             negM_ready, consts_ready):
    qsb = [b.arena(0, SEQ), b.arena(SEQ, SEQ)]
    ksb = [b.arena(2 * SEQ, SEQ), b.arena(3 * SEQ, SEQ)]
    vts = [b.work(0, SEQ), b.work(SEQ, SEQ)]
    vsb = b.work(2 * SEQ, SEQ)
    psl = [b.sb(f"apl{i}", [128, 512], BF16) for i in range(2)]
    ost = [b.sb(f"ao{i}", [128, 512], BF16) for i in range(2)]
    lsb = b.sb("al", [1, 512], F32)
    rlb = b.sb("arl", [1, 512], BF16)
    rbc = [b.sb(f"arb{i}", [128, 512], BF16) for i in range(2)]
    ps = [b.t[f"ps{i}"] for i in range(4)]
    psb = b.t["ps4"]
    pvb = [b.t["ps2"], b.t["psE"]]
    llb = [b.t["ps3"], b.t["psF"]]
    trib = b.t["trib"]
    s_ld = [p.sem(f"s_ald{tag}0"), p.sem(f"s_ald{tag}1")]
    s_st = [p.sem(f"s_ast{tag}0"), p.sem(f"s_ast{tag}1")]
    s_cc = p.sem("s_occ" + tag)
    s_pe, s_dv, s_ac = p.sem("s_pe"), p.sem("s_dv"), p.sem("s_ac")

    unit_pe_done = {}
    vcopy_done = {}
    last_eo_dv = None
    for u in range(NU):
        par = u % 2
        w = list(unit_ready[u]) + consts_ready + negM_ready
        if u >= 2:
            w.append((s_pe, unit_pe_done[u - 2]))
        p.dma("sp", qsb[par][:], qkvT_ap[3 * u * 128:(3 * u + 1) * 128, :],
              s_ld[par], waits=w)
        p.dma("sp", ksb[par][:],
              qkvT_ap[(3 * u + 1) * 128:(3 * u + 2) * 128, :], s_ld[par])
        l3 = p.dma("sp", vts[par][:],
                   qkvT_ap[(3 * u + 2) * 128:(3 * u + 3) * 128, :],
                   s_ld[par])
        ldw = [(s_ld[par], l3)]
        vc_w = list(ldw) + consts_ready
        if u >= 1:
            vc_w.append((s_pe, unit_pe_done[u - 1]))
        if last_eo_dv is not None:
            vc_w.append((s_dv, last_eo_dv))
        for g in range(8):
            bank = b.t["ptrA"] if g % 2 == 0 else b.t["ptrB"]
            wts = list(vc_w)
            if (u, g - 2) in vcopy_done:
                wts.append((s_dv, vcopy_done[(u, g - 2)]))
            tv = None
            for i in range(8):
                tile = g * 8 + i
                tv = p.op("pe", "transpose", bank[:, i * 128:(i + 1) * 128],
                          vts[par][:, tile * 128:(tile + 1) * 128],
                          b.t["idb"][:], waits=wts,
                          inc=s_pe if i == 7 else None)
            vcopy_done[(u, g)] = p.op(
                "dv", "tensor_copy", vsb[:, g * 1024:(g + 1) * 1024],
                bank[:, 0:1024], waits=[(s_pe, tv)], inc=s_dv)
        v_ready = [(s_dv, vcopy_done[(u, 7)])]

        exp_done = {}
        msk_done = {}
        eo_marks = {}
        el_marks = {}
        ebc_prev = None
        for qc in range(16):
            nkt = 4 * qc + 4 if causal else 64
            qpar = qc % 2
            pvB, llB = pvb[qpar], llb[qpar]

            def emit_lpv(pj, poff, first, last, qc=qc, llB=llB, pvB=pvB):
                wp = [(s_dv, msk_done[(qc, pj)])] if (qc, pj) in msk_done \
                    else [(s_ac, exp_done[(qc, pj)])]
                wpl = list(wp)
                if first and (qc - 2) in el_marks:
                    wpl.append((s_dv, el_marks[qc - 2]))
                p.op("pe", lambda t, pj=pj, poff=poff, first=first,
                     last=last, llB=llB:
                     t.matmul(llB[0:1, poff:512], b.t["ones128"][:],
                              psl[pj % 2][:, poff:512],
                              start=first, stop=last),
                     waits=wpl, inc=s_pe)
                wpv = list(wp)
                if first and (qc - 2) in eo_marks:
                    wpv.append((s_dv, eo_marks[qc - 2]))
                return p.op(
                    "pe", lambda t, pj=pj, poff=poff, first=first,
                    last=last, pvB=pvB:
                    t.matmul(pvB[:, poff:512],
                             vsb[:, pj * 128:(pj + 1) * 128],
                             psl[pj % 2][:, poff:512],
                             start=first, stop=last),
                    waits=wpv, inc=s_pe)

            pend = None
            last_unit_mm = None
            for j in range(nkt):
                off = (max(0, j - 4 * qc) if causal else 0) * 128
                sl = j % 2
                sbank = ps[sl]
                w2 = ldw + v_ready
                if (qc, j - 2) in exp_done:
                    w2.append((s_ac, exp_done[(qc, j - 2)]))

                def smm(t, sbank=sbank, j=j, off=off, par=par, qc=qc):
                    return t.matmul(
                        sbank[:, off:512],
                        ksb[par][:, j * 128:(j + 1) * 128],
                        qsb[par][:, qc * 512 + off:(qc + 1) * 512],
                        start=True, stop=True)
                ts = p.op("pe", smm, waits=w2, inc=s_pe)
                te = p.op("ac", "activation", psl[sl][:, off:512],
                          sbank[:, off:512], AF.Exp, scale=SCALE,
                          bias=negM[:, u: u + 1],
                          waits=[(s_pe, ts)] + negM_ready, inc=s_ac)
                exp_done[(qc, j)] = te
                if causal and 0 <= j - 4 * qc < 4:
                    msk_done[(qc, j)] = p.op(
                        "dv", "tensor_mul", psl[sl][:, off: off + 128],
                        psl[sl][:, off: off + 128], trib[:],
                        waits=[(s_ac, te)] + consts_ready, inc=s_dv)
                if pend is not None:
                    last_unit_mm = emit_lpv(pend[0], pend[1],
                                            pend[0] == 0, False)
                pend = (j, off)
            last_unit_mm = emit_lpv(pend[0], pend[1], pend[0] == 0, True)

            el = p.op("dv", "tensor_copy", lsb[:], llB[0:1, 0:512],
                      waits=[(s_pe, last_unit_mm)], inc=s_dv)
            el_marks[qc] = el
            er = p.op("dv", "reciprocal", lsb[:], lsb[:],
                      waits=[(s_dv, el)], inc=s_dv)
            eb = p.op("dv", "tensor_copy", rlb[:], lsb[:],
                      waits=[(s_dv, er)], inc=s_dv)
            wb6 = [(s_dv, eb)]
            if ebc_prev is not None:
                wb6.append((s_dv, ebc_prev))
            tb6 = p.op("pe", lambda t: t.matmul(
                psb[:, 0:512], b.t["ones1"][0:1, 0:128], rlb[0:1, :],
                start=True, stop=True), waits=wb6, inc=s_pe)
            rslot = qc % 2
            ebc = p.op("dv", "tensor_copy", rbc[rslot][:], psb[:, 0:512],
                       waits=[(s_pe, tb6)], inc=s_dv)
            ebc_prev = ebc
            oslot = qc % 2
            w3 = [(s_dv, ebc)]
            if p.sems[f"s_ast{tag}{oslot}"].count > 0:
                w3.append((s_st[oslot], p.sems[f"s_ast{tag}{oslot}"].count))
            eo = p.op("dv", "scalar_tensor_tensor", ost[oslot][:],
                      pvB[:, 0:512], 1.0, rbc[rslot][:],
                      ALU.mult, ALU.mult, waits=w3, inc=s_dv)
            eo_marks[qc] = eo
            p.dma("sp", ot_ts[u].ap()[:, qc * 512:(qc + 1) * 512],
                  ost[oslot][:], s_st[oslot], waits=[(s_dv, eo)])
        unit_pe_done[u] = p.sems["s_pe"].count
        last_eo_dv = p.sems["s_dv"].count
        stw = [(s_st[0], p.sems[f"s_ast{tag}0"].count),
               (s_st[1], p.sems[f"s_ast{tag}1"].count)]

        def fn(g, u=u):
            return g.collective_compute(
                "AllGather", ALU.bypass, replica_groups=RG,
                ins=[ot_ts[u].ap().opt()], outs=[og_ts[u].ap().opt()])
        if p.last_cc is not None:
            stw = stw + [p.last_cc]
        mark = p.op("gp", fn, waits=stw, inc=s_cc)
        p.last_cc = (s_cc, mark)
    return [(s_cc, p.sems["s_occ" + tag].count)]


def emit_ot_ag(p, b, tag, ot_t, og_t, att_done):
    s_cc = p.sem("s_occ" + tag)

    def fn(g):
        return g.collective_compute(
            "AllGather", ALU.bypass, replica_groups=RG,
            ins=[ot_t.ap().opt()], outs=[og_t.ap().opt()])
    v = p.op("gp", fn, waits=att_done, inc=s_cc)
    return [(s_cc, v)]


def emit_proj(p, b, tag, og_aps, wdense_ap, bdense_ap, x_ready, consts_ready,
              og_ready):
    x_sb = b.t["x_sb"]
    osb = b.work(0, 4 * TOK)
    dbf = [b.work(4 * TOK + i * D, D) for i in range(4)]
    bq = b.work(16384, D, nrow=1)
    ps = [b.t[f"ps{i}"] for i in range(4)]
    s_o = p.sem(f"s_pjo{tag}")
    s_wd = [p.sem(f"s_pjw{tag}{i}") for i in range(4)]
    s_b = p.sem(f"s_pjb{tag}")
    s_pe, s_dv = p.sem("s_pe"), p.sem("s_dv")

    bd = p.dma("gp", bq, bdense_ap, s_b,
               waits=consts_ready + x_ready + og_ready)
    prev_mm = None
    prev_ev = None
    for hg in range(4):
        w = ([(s_pe, prev_mm)] if prev_mm else []) + og_ready

        def ogload(engine, handles, hg=hg):
            for arm in engine.switch_core_id(n=128):
                cc = arm.logical % 4
                for i in range(4):
                    engine.dma_start(
                        out=osb[:, i * TOK:(i + 1) * TOK],
                        in_=og_aps[i][hg * 128:(hg + 1) * 128,
                                      cc * TOK:(cc + 1) * TOK]
                    ).then_inc(handles["s_pjo" + tag], 16)
            return None
        ogload.wants_handles = True
        p.op("sp", ogload, waits=w)
        s_o.count += 64
        olast = s_o.count
        wlast = []
        for i in range(4):
            h = hg * 4 + i
            wlast.append(p.dma("gp", dbf[i], wdense_ap[h * 128:(h + 1) * 128, :],
                               s_wd[i],
                               waits=([(s_pe, prev_mm)] if prev_mm
                                      else og_ready)))
        ow = [(s_o, olast)] + [(s_wd[i], wlast[i]) for i in range(4)]
        for tb in range(TB):
            w2 = ow + [(s_b, bd)] + x_ready
            if prev_ev is not None:
                w2.append((s_dv, prev_ev))

            def mmfn(t, hg=hg, tb=tb):
                last = None
                for dcol in range(4):
                    if hg == 0:
                        t.matmul(ps[dcol][:], b.t["ones1"][0:1, 0:128],
                                 bq[0:1, dcol * 512:(dcol + 1) * 512],
                                 start=True, stop=False)
                for i in range(4):
                    for dcol in range(4):
                        last = t.matmul(
                            ps[dcol][:],
                            osb[:, i * TOK + tb * 128: i * TOK + (tb + 1) * 128],
                            dbf[i][:, dcol * 512:(dcol + 1) * 512],
                            start=(hg != 0 and i == 0), stop=(i == 3))
                return last
            prev_mm = p.op("pe", mmfn, waits=w2, inc=s_pe)
            for dcol in range(4):
                xs = x_sb[:, tb * D + dcol * 512: tb * D + (dcol + 1) * 512]
                prev_ev = p.op("dv", "tensor_add", xs, xs, ps[dcol][:],
                               waits=[(s_pe, prev_mm)], inc=s_dv)
    return [(s_dv, p.sems["s_dv"].count)]



def build_fused(upto="all"):
    nc = bass.Bass("TRN2", target_bir_lowering=False, debug=False)
    x_in = nc.dram_tensor("x_in", [TB, 128, D], F32, kind="ExternalInput").ap()
    gate_wT = nc.dram_tensor("gate_wT", [8, D], F32, kind="ExternalInput").ap()
    gate_b = nc.dram_tensor("gate_b", [1, 8], F32, kind="ExternalInput").ap()
    exp_w = nc.dram_tensor("exp_w", [E0, D, D], F32, kind="ExternalInput").ap()
    exp_b = nc.dram_tensor("exp_b", [E0, D], F32, kind="ExternalInput").ap()
    wqkv0 = nc.dram_tensor("wqkv0", [D, 1536], F32, kind="ExternalInput").ap()
    bqkv0 = nc.dram_tensor("bqkv0", [1, 1536], F32, kind="ExternalInput").ap()
    wdense0 = nc.dram_tensor("wdense0", [D, D], F32, kind="ExternalInput").ap()
    bdense0 = nc.dram_tensor("bdense0", [1, D], F32, kind="ExternalInput").ap()
    ct_t = nc.dram_tensor("ct", [16, SEQ], F32, kind="ExternalInput").ap()
    st_t = nc.dram_tensor("st", [16, SEQ], F32, kind="ExternalInput").ap()
    rout_w = nc.dram_tensor("rout_w", [8, D], F32, kind="ExternalInput").ap()
    f1wg = nc.dram_tensor("f1wg", [E0, D, H1], F32, kind="ExternalInput").ap()
    f1wu = nc.dram_tensor("f1wu", [E0, D, H1], F32, kind="ExternalInput").ap()
    f1wd = nc.dram_tensor("f1wd", [E0, H1, D], F32, kind="ExternalInput").ap()
    f1sg = nc.dram_tensor("f1sg", [D, HS], F32, kind="ExternalInput").ap()
    f1su = nc.dram_tensor("f1su", [D, HS], F32, kind="ExternalInput").ap()
    f1sd = nc.dram_tensor("f1sd", [HS, D], F32, kind="ExternalInput").ap()
    wqkv1 = nc.dram_tensor("wqkv1", [D, 1536], F32, kind="ExternalInput").ap()
    bqkv1 = nc.dram_tensor("bqkv1", [1, 1536], F32, kind="ExternalInput").ap()
    wproj1 = nc.dram_tensor("wproj1", [D, D], F32, kind="ExternalInput").ap()
    bproj1 = nc.dram_tensor("bproj1", [1, D], F32, kind="ExternalInput").ap()
    consts = nc.dram_tensor("consts", [128, 384], F32, kind="ExternalInput").ap()
    x_out = nc.dram_tensor("x_out", [TB, 128, D], F32, kind="ExternalOutput").ap()

    xlc = [[nc.dram_tensor(f"xl{i}_{cp}", [D, 512], BF16) for cp in range(4)]
           for i in range(2)]
    xgc = [[nc.dram_tensor(f"xg{i}_{cp}", [4 * D, 512], BF16)
            for cp in range(4)] for i in range(2)]
    qkvT = [nc.dram_tensor(f"qkvT{i}", [1536, SEQ], BF16) for i in range(2)]
    ot = [[nc.dram_tensor(f"ot{i}_{u}", [128, SEQ], BF16) for u in range(NU)]
          for i in range(2)]
    og = [[nc.dram_tensor(f"og{i}_{u}", [512, SEQ], BF16) for u in range(NU)]
          for i in range(2)]

    with ExitStack() as stack:
        p, b = Prog(nc), Bufs(nc, stack)
        alloc_common(p, b)
        cr = emit_consts(p, b, consts)
        xr = emit_load_x(p, b, x_in)
        fin = []

        ln1 = emit_ln(p, b, xr, cr)
        wg0, gr0 = emit_gating(p, b, "ffn0", "a", gate_wT, gate_b, ln1, cr)
        f0 = emit_ffn0(p, b, "a", exp_w, exp_b, wg0, gr0, ln1)
        if upto == "ffn0":
            stx = emit_store_x(p, b, x_out, f0)
            p.finalize(stack, stx)
            return nc
        ln2 = emit_ln(p, b, f0, cr)
        ag0 = emit_xnt_ag(p, b, "0", xlc[0], xgc[0], ln2)
        qk0, nM0, negM0 = emit_qkv_units(
            p, b, "0", True, wqkv0, bqkv0, [t.ap() for t in xgc[0]],
            qkvT[0].ap(), ct_t, st_t, ag0, f0)
        if upto == "qkv0":
            dbg = nc.dram_tensor("dbg_qkvT", [1536, SEQ], BF16,
                                 kind="ExternalOutput").ap()
            dbgn = nc.dram_tensor("dbg_negm", [1, NU], F32,
                                  kind="ExternalOutput").ap()
            s_dbg = p.sem("s_dbg")
            dl = None
            for fb in range(12):
                dl = p.dma("gp", dbg[fb * 128:(fb + 1) * 128, :],
                           qkvT[0].ap()[fb * 128:(fb + 1) * 128, :],
                           s_dbg, waits=qk0[-1] + nM0)
            dl2 = p.dma("gp", dbgn, b.t["negm0"][:], s_dbg)
            stx = emit_store_x(p, b, x_out, f0)
            p.finalize(stack, stx + [(s_dbg, dl2)])
            return nc
        oag0 = emit_att(p, b, "0", True, qkvT[0].ap(), ot[0], og[0], negM0,
                        qk0, nM0, cr)
        pj0 = emit_proj(p, b, "0", [t.ap() for t in og[0]], wdense0, bdense0,
                        ln2, cr, oag0)
        if upto == "att0":
            stx = emit_store_x(p, b, x_out, pj0)
            p.finalize(stack, stx)
            return nc
        ln3 = emit_ln(p, b, pj0, cr)
        wg0b, gr0b = emit_gating(p, b, "ffn0", "a", gate_wT, gate_b, ln3, cr)
        f0b = emit_ffn0(p, b, "a", exp_w, exp_b, wg0b, gr0b, ln3)
        ln4 = emit_ln(p, b, f0b, cr)
        wg1, gr1 = emit_gating(p, b, "ffn1", "b", rout_w, None, ln4, cr)
        experts = [(f1wg[e], f1wu[e], f1wd[e], ("col", wg1, e))
                   for e in range(E0)]
        experts.append((f1sg[:, 0:H1], f1su[:, 0:H1], f1sd[0:H1, :],
                        ("const", 0.5)))
        experts.append((f1sg[:, H1:HS], f1su[:, H1:HS], f1sd[H1:HS, :],
                        ("const", 0.5)))
        f1 = emit_ffn1(p, b, "a", experts, ln4 + gr1)
        if upto == "ffn1":
            stx = emit_store_x(p, b, x_out, f1)
            p.finalize(stack, stx)
            return nc
        ln5 = emit_ln(p, b, f1, cr)
        ag1 = emit_xnt_ag(p, b, "1", xlc[1], xgc[1], ln5)
        qk1, nM1, negM1 = emit_qkv_units(
            p, b, "1", False, wqkv1, bqkv1, [t.ap() for t in xgc[1]],
            qkvT[1].ap(), None, None, ag1, f1)
        oag1 = emit_att(p, b, "1", False, qkvT[1].ap(), ot[1], og[1], negM1,
                        qk1, nM1, cr)
        pj1 = emit_proj(p, b, "1", [t.ap() for t in og[1]], wproj1, bproj1,
                        ln5, cr, oag1)
        ln6 = emit_ln(p, b, pj1, cr)
        wg1b, gr1b = emit_gating(p, b, "ffn1", "b", rout_w, None, ln6, cr)
        experts2 = [(f1wg[e], f1wu[e], f1wd[e], ("col", wg1b, e))
                    for e in range(E0)]
        experts2.append((f1sg[:, 0:H1], f1su[:, 0:H1], f1sd[0:H1, :],
                         ("const", 0.5)))
        experts2.append((f1sg[:, H1:HS], f1su[:, H1:HS], f1sd[H1:HS, :],
                         ("const", 0.5)))
        f1b = emit_ffn1(p, b, "a", experts2, ln6 + gr1b)
        stx = emit_store_x(p, b, x_out, f1b)
        p.finalize(stack, stx)
    return nc



def _consts_host():
    c = np.zeros((128, 384), np.float32)
    c[:, 0:128] = np.eye(128)
    kk, qq = np.mgrid[0:128, 0:128]
    c[:, 128:256] = (kk <= qq).astype(np.float32)
    c[:, 256:384] = np.eye(128)
    return c


def _rope_tables():
    invf = 1.0 / (10000.0 ** (np.arange(0, 32, 2, dtype=np.float64) / 32))
    f = np.arange(SEQ, dtype=np.float64)[:, None] * invf[None, :]
    return (np.ascontiguousarray(np.cos(f).T.astype(np.float32)),
            np.ascontiguousarray(np.sin(f).T.astype(np.float32)))


_NC_CACHE = {}


def _get_nc(name, builder):
    if name not in _NC_CACHE:
        _NC_CACHE[name] = builder()
    return _NC_CACHE[name]


def kernel(L0_attn_hidden_states, L0_attn_W_qkv, L0_attn_b_qkv,
           L0_attn_W_dense, L0_attn_b_dense, L0_ffn_gate_w, L0_ffn_gate_b,
           L0_ffn_expert_w, L0_ffn_expert_b, L1_attn_w_qkv, L1_attn_b_qkv,
           L1_attn_w_proj, L1_attn_b_proj, L1_ffn_router_w, L1_ffn_wg,
           L1_ffn_wu, L1_ffn_wd, L1_ffn_sg, L1_ffn_su, L1_ffn_sd):
    consts = _consts_host()
    ct, st = _rope_tables()
    cores = list(range(8))
    f32 = lambda a: np.ascontiguousarray(np.asarray(a), dtype=np.float32)

    x = np.asarray(L0_attn_hidden_states)
    W0 = f32(L0_attn_W_qkv)
    b0 = f32(L0_attn_b_qkv)
    W1 = f32(L1_attn_w_qkv)
    b1 = f32(L1_attn_b_qkv)

    def unit_cols(layer, h):
        if layer == 0:
            return list(range(h * 384, (h + 1) * 384))
        return (list(range(h * 128, (h + 1) * 128)) +
                list(range(2048 + h * 128, 2048 + (h + 1) * 128)) +
                list(range(4096 + h * 128, 4096 + (h + 1) * 128)))

    nc = _get_nc("fused", build_fused)
    in_maps = []
    for c in cores:
        bb, cc = divmod(c, 4)
        heads = [4 * cc + i for i in range(NU)]
        cols0 = sum((unit_cols(0, h) for h in heads), [])
        cols1 = sum((unit_cols(1, h) for h in heads), [])
        in_maps.append({
            "x_in": np.ascontiguousarray(
                x[bb, cc * TOK:(cc + 1) * TOK].reshape(TB, 128, D)),
            "gate_wT": f32(np.asarray(L0_ffn_gate_w).T),
            "gate_b": f32(L0_ffn_gate_b).reshape(1, 8),
            "exp_w": f32(L0_ffn_expert_w),
            "exp_b": f32(L0_ffn_expert_b),
            "wqkv0": np.ascontiguousarray(W0[:, cols0]),
            "bqkv0": np.ascontiguousarray(b0[cols0]).reshape(1, -1),
            "wdense0": f32(L0_attn_W_dense),
            "bdense0": f32(L0_attn_b_dense).reshape(1, -1),
            "ct": ct, "st": st,
            "rout_w": f32(L1_ffn_router_w),
            "f1wg": f32(L1_ffn_wg), "f1wu": f32(L1_ffn_wu),
            "f1wd": f32(L1_ffn_wd),
            "f1sg": f32(L1_ffn_sg), "f1su": f32(L1_ffn_su),
            "f1sd": f32(L1_ffn_sd),
            "wqkv1": np.ascontiguousarray(W1[:, cols1]),
            "bqkv1": np.ascontiguousarray(b1[cols1]).reshape(1, -1),
            "wproj1": f32(L1_attn_w_proj),
            "bproj1": f32(L1_attn_b_proj).reshape(1, -1),
            "consts": consts,
        })
    res = run_bass_kernel_spmd(nc, in_maps, core_ids=cores)
    out = np.empty((B, SEQ, D), np.float32)
    for c in cores:
        bb, cc = divmod(c, 4)
        out[bb, cc * TOK:(cc + 1) * TOK] = \
            np.asarray(res.results[c]["x_out"]).reshape(TOK, D)
    return out
